# version 01
# baseline (speedup 1.0000x reference)
import numpy as np
import ml_dtypes
from contextlib import ExitStack, contextmanager
import concourse.bass as bass
import concourse.mybir as mybir
from concourse.bass_utils import run_bass_kernel_spmd

F32 = mybir.dt.float32
BF16 = mybir.dt.bfloat16
I32 = mybir.dt.int32
AF = mybir.ActivationFunctionType
ALU = mybir.AluOpType
AX = mybir.AxisListType

D = 1024
T = 2048
NR = 2
TT = NR * T
NCH = 8
BFD = ml_dtypes.bfloat16

DMA_SEMS = ("LY", "LG", "ST", "LX", "LW", "LH", "LQ", "SH")
ALL_SEMS = ("T", "V", "S", "G", "LY", "LG", "ST", "LX", "LW", "LH", "LQ", "SH")
import os as _os
DBG = _os.environ.get("KDBG", "") == "1"



class Prog:
    def __init__(self):
        self.nc = bass.Bass("TRN2", target_bir_lowering=False, detect_race_conditions=False)
        self.es = ExitStack()
        self.ops = {k: [] for k in "tvsgy"}
        self.sems = {}
        self.cnt = {}
        self.hwm = {k: {} for k in "tvsgy"}
        for s in ALL_SEMS:
            self.sems[s] = self.es.enter_context(self.nc.semaphore(s))
            self.cnt[s] = 0
        self._built = False
        self.auto_sp = False
        self.cur_es = self.es
        self.psA = None

    @contextmanager
    def scope(self):
        st = ExitStack()
        old = self.cur_es
        self.cur_es = st
        try:
            yield
        finally:
            st.close()
            self.cur_es = old

    def alloc_spacers(self):
        self.jV = self.sbuf("prog_jV", [128, 256], F32)
        self.jS = self.sbuf("prog_jS", [128, 256], F32)
        self.op("v", lambda E: E.memset(self.jV[:], 0.0))
        self.op("v", lambda E: E.memset(self.jS[:], 0.0))

    def sp(self, e, n=2):
        for _ in range(n):
            if e == "v":
                self.op("v", lambda E: E.tensor_copy(self.jV[:], self.jV[:]))
            elif e == "s":
                self.op("s", lambda E: E.activation(self.jS[:], self.jS[:], AF.Square))

    def spt(self, e="v", n=2):
        self.sp(e, n)
        if e == "v":
            return self.op("v", lambda E: E.tensor_copy(self.jV[:], self.jV[:]), inc="V")
        return self.op("s", lambda E: E.activation(self.jS[:], self.jS[:], AF.Square),
                       inc="S")

    def alloc_psum_pool(self):
        self.psA = self.es.enter_context(self.nc.psum_tensor("psA", [128, 512], F32))
        self.psB = self.es.enter_context(self.nc.psum_tensor("psB", [128, 512], F32))
        self.psC = self.es.enter_context(self.nc.psum_tensor("psC", [128, 512], F32))
        self.psD = self.es.enter_context(self.nc.psum_tensor("psD", [128, 512], F32))
        self.psE = self.es.enter_context(self.nc.psum_tensor("psE", [128, 512], F32))
        self.psF = self.es.enter_context(self.nc.psum_tensor("psF", [128, 512], F32))
        self.psT = self.es.enter_context(self.nc.psum_tensor("psT", [128, 512], BF16))

    def sbuf(self, name, shape, dt=F32):
        return self.cur_es.enter_context(self.nc.sbuf_tensor(name, shape, dt))

    def dram(self, name, shape, dt=F32, kind="Internal"):
        return self.nc.dram_tensor(name, shape, dt, kind=kind).ap()

    def wait(self, e, tok):
        if tok is None:
            return
        s, v = tok
        if s in DMA_SEMS:
            v = self.cnt[s]
        if v <= 0 or self.hwm[e].get(s, 0) >= v:
            return
        self.hwm[e][s] = v
        self.ops[e].append(lambda E, s=s, v=v: E.wait_ge(self.sems[s], v))

    def op(self, e, build, inc=None, waits=()):
        for w in waits:
            self.wait(e, w)
        if inc:
            amt = 16 if inc in DMA_SEMS else 1
            self.cnt[inc] += amt
            self.ops[e].append(lambda E, b=build, s=inc, a=amt: b(E).then_inc(self.sems[s], a))
            tk = (inc, self.cnt[inc])
        else:
            self.ops[e].append(build)
            tk = None
        if self.auto_sp and e == "v":
            self.ops["v"].append(lambda E: E.tensor_copy(self.jV[:], self.jV[:]))
        return tk

    def tok(self, sem):
        return (sem, self.cnt[sem])

    def dbb(self, q, sem):
        self.wait(q, (sem, self.cnt[sem]))

    def dbg(self, name, ap, shape, dt=F32, dram_src=None):
        if not DBG:
            return
        d = self.dram("dbg_" + name, list(shape), dt, kind="ExternalOutput")
        toks = [(s, self.cnt[s]) for s in ALL_SEMS]
        if dram_src is not None:
            dr, r = dram_src
            self.op("y", lambda E: E.dma_start(d[:, 0:T], dr[r, 0:128, :]), inc="ST",
                    waits=toks)
        else:
            self.op("y", lambda E: E.dma_start(d[:, :], ap), inc="ST", waits=toks)

    def barrier(self):
        toks = [(s, self.cnt[s]) for s in ALL_SEMS]
        for e in "tvsgy":
            for t_ in toks:
                self.wait(e, t_)

    def build(self):
        assert not self._built
        self._built = True
        self.nc.all_engine_barrier()
        self.nc.gpsimd.dma_reset(range(153, 256))
        self.nc.gpsimd.sem_clear(range(153, 256))
        self.nc.all_engine_barrier()
        with self.es:
            block = self.es.enter_context(self.nc.Block())

            @block.tensor
            def _(E):
                for fn in self.ops["t"]:
                    fn(E)

            @block.vector
            def _(E):
                for fn in self.ops["v"]:
                    fn(E)

            @block.scalar
            def _(E):
                for fn in self.ops["s"]:
                    fn(E)

            @block.gpsimd
            def _(E):
                for fn in self.ops["g"]:
                    fn(E)

            @block.sync
            def _(E):
                for fn in self.ops["y"]:
                    fn(E)
        return self.nc


def host_constants():
    c = {}
    c["c_ident"] = np.eye(128, dtype=np.float32)
    c["c_ones"] = np.ones((128, 1), np.float32)
    half = 16
    inv = 1.0 / 10000.0 ** (np.arange(half, dtype=np.float32) / half)
    ang = np.arange(T, dtype=np.float32)[:, None] * inv[None, :]
    cos, sin = np.cos(ang).T, np.sin(ang).T
    cosf = np.zeros((128, T), np.float32)
    sinf = np.zeros((128, T), np.float32)
    for g in range(4):
        cosf[g * 32:g * 32 + 16] = cos
        cosf[g * 32 + 16:g * 32 + 32] = cos
        sinf[g * 32:g * 32 + 16] = -sin
        sinf[g * 32 + 16:g * 32 + 32] = sin
    c["c_cosf"] = cosf
    c["c_sinf"] = sinf
    masks = np.zeros((4, 128, 512), np.float32)
    j = np.arange(512)[None, :]
    p = np.arange(128)[:, None]
    for dv in range(4):
        masks[dv] = np.where(j <= dv * 128 + p, 0.0, -1e9).astype(np.float32)
    c["c_masks"] = masks.reshape(4 * 128, 512)
    return c


def declare_consts(P, l0=True):
    C = {}
    d_ident = P.dram("c_ident", [128, 128], F32, kind="ExternalInput")
    d_ones = P.dram("c_ones", [128, 1], F32, kind="ExternalInput")

    C["identb"] = P.sbuf("cs_identb", [128, 128], BF16)
    C["identf"] = P.sbuf("cs_identf", [128, 128], F32)
    if l0:
        d_cosf = P.dram("c_cosf", [128, T], F32, kind="ExternalInput")
        d_sinf = P.dram("c_sinf", [128, T], F32, kind="ExternalInput")
        d_masks = P.dram("c_masks", [4 * 128, 512], F32, kind="ExternalInput")
        C["cosf"] = P.sbuf("cs_cosf", [128, T], BF16)
        C["sinf"] = P.sbuf("cs_sinf", [128, T], BF16)
        C["masks"] = P.sbuf("cs_masks", [128, 4 * 512], BF16)
    C["ones128_bf"] = P.sbuf("cs_onesb", [128, 1], BF16)
    C["ones128_f"] = P.sbuf("cs_onesf", [128, 1], F32)
    C["ones1x128"] = P.sbuf("cs_ones1x", [1, 128], F32)

    P.op("g", lambda E: E.dma_start(C["identb"][:], d_ident[:]), inc="LG")
    P.op("y", lambda E: E.dma_start(C["identf"][:], d_ident[:]), inc="LY")
    if l0:
        P.op("g", lambda E: E.dma_start(C["cosf"][:], d_cosf[:]), inc="LG")
        P.op("g", lambda E: E.dma_start(C["sinf"][:], d_sinf[:]), inc="LG")
        for i in range(4):
            P.op("g", lambda E, i=i: E.dma_start(C["masks"][:, i * 512:(i + 1) * 512],
                                                 d_masks[i * 128:(i + 1) * 128, :]), inc="LG")
    P.op("g", lambda E: E.dma_start(C["ones128_bf"][:], d_ones[:]), inc="LG")
    P.op("y", lambda E: E.dma_start(C["ones128_f"][:], d_ones[:]), inc="LY")
    P.op("y", lambda E: E.dma_start(C["ones1x128"][:], d_ones.rearrange("p o -> o p")), inc="LY")
    C["ready"] = [P.tok("LG"), P.tok("LY")]
    return C



def emit_post_ln(P, C, zin, xout, tag):
    zf = P.sbuf(f"{tag}_zf", [128, 512], F32)
    zf2 = P.sbuf(f"{tag}_zf2", [128, 512], F32)
    zbf = P.sbuf(f"{tag}_zbf", [128, 512], BF16)
    z2bf = P.sbuf(f"{tag}_z2bf", [128, 512], BF16)
    sums = P.sbuf(f"{tag}_sums", [1, TT], F32)
    sumsq = P.sbuf(f"{tag}_sumsq", [1, TT], F32)
    mean = P.sbuf(f"{tag}_mean", [1, TT], F32)
    rstd = P.sbuf(f"{tag}_rstd", [1, TT], F32)
    tmp1 = P.sbuf(f"{tag}_tmp1", [1, TT], F32)
    ot = P.sbuf(f"{tag}_ot", [128, 512], F32)
    onesb = C["ones128_bf"]
    ones1x = C["ones1x128"]
    ps1, ps2, pb1, pb2 = P.psA, P.psB, P.psC, P.psD

    P.barrier()
    ntile = TT // 512
    for it in range(ntile):
        r, tl = divmod(it, T // 512)
        tl *= 512
        mm = None
        for ch in range(NCH):
            ld = P.op("y", lambda E, r=r, ch=ch, tl=tl:
                      E.dma_start(zf[:], zin[r, ch * 128:(ch + 1) * 128, tl:tl + 512]),
                      inc="LY", waits=[P.tok("V"), P.tok("S")])
            cv = P.op("v", lambda E: E.tensor_copy(zbf[:], zf[:]), inc="V",
                      waits=[ld, P.tok("T")])
            cs = P.op("s", lambda E: E.activation(z2bf[:], zf[:], AF.Square), inc="S",
                      waits=[ld, P.tok("T")])
            P.op("t", lambda E, ch=ch: E.matmul(ps1[0:1, :], onesb[:, 0:1], zbf[:],
                                                start=(ch == 0), stop=(ch == NCH - 1)),
                 waits=[cv])
            mm = P.op("t", lambda E, ch=ch: E.matmul(ps2[0:1, :], onesb[:, 0:1], z2bf[:],
                                                     start=(ch == 0), stop=(ch == NCH - 1)),
                      inc="T", waits=[cs])
        P.op("v", lambda E, it=it: E.tensor_copy(sums[:, it * 512:(it + 1) * 512], ps1[0:1, :]),
             waits=[mm])
        P.op("v", lambda E, it=it: E.tensor_copy(sumsq[:, it * 512:(it + 1) * 512], ps2[0:1, :]),
             inc="V")
    t1 = P.op("v", lambda E: E.tensor_scalar(mean[:], sums[:], 1.0 / D, None, ALU.mult), inc="V")
    t2 = P.op("s", lambda E: E.activation(tmp1[:], mean[:], AF.Square), inc="S", waits=[t1])
    P.op("v", lambda E: E.scalar_tensor_tensor(tmp1[:], sumsq[:], 1.0 / D, tmp1[:],
                                               ALU.mult, ALU.subtract), waits=[t2])
    t4 = P.op("v", lambda E: E.tensor_scalar(tmp1[:], tmp1[:], 1e-5, None, ALU.add), inc="V")
    t5 = P.op("s", lambda E: E.activation(tmp1[:], tmp1[:], AF.Sqrt), inc="S", waits=[t4])
    t6 = P.op("v", lambda E: E.reciprocal(rstd[:], tmp1[:]), inc="V", waits=[t5])

    last = None
    for it in range(ntile):
        r, tl = divmod(it, T // 512)
        tl *= 512
        P.op("t", lambda E, it=it: E.matmul(pb1[:], ones1x[0:1, :],
                                            mean[0:1, it * 512:(it + 1) * 512],
                                            start=True, stop=True),
             waits=[t6, P.tok("V")])
        b2 = P.op("t", lambda E, it=it: E.matmul(pb2[:], ones1x[0:1, :],
                                                 rstd[0:1, it * 512:(it + 1) * 512],
                                                 start=True, stop=True), inc="T")
        for ch in range(NCH):
            ld = P.op("y", lambda E, r=r, ch=ch, tl=tl:
                      E.dma_start(zf2[:], zin[r, ch * 128:(ch + 1) * 128, tl:tl + 512]),
                      inc="LY", waits=[P.tok("V")])
            a1 = P.op("v", lambda E: E.tensor_tensor(ot[:], zf2[:], pb1[:], ALU.subtract),
                      waits=[ld, b2, P.tok("ST")])
            a2 = P.op("v", lambda E: E.tensor_tensor(ot[:], ot[:], pb2[:], ALU.mult), inc="V")
            last = P.op("y", lambda E, r=r, ch=ch, tl=tl:
                        E.dma_start(xout[r, ch * 128:(ch + 1) * 128, tl:tl + 512], ot[:]),
                        inc="ST", waits=[a2])
    return last



def prep_l0_attn_weights(d):
    w = {}
    w["l0a_Wq"] = np.ascontiguousarray(d["L0_attn_Wq"])
    Wqr = d["L0_attn_Wqr"]
    perm = []
    for h in range(16):
        b = h * 32
        perm += list(range(b + 16, b + 32)) + list(range(b, b + 16))
    w["l0a_Wqr_nat"] = np.ascontiguousarray(Wqr)
    w["l0a_Wqr_swp"] = np.ascontiguousarray(Wqr[:, perm])
    Wkr = d["L0_attn_Wkr"]
    w["l0a_Wkr_nat"] = np.ascontiguousarray(Wkr)
    w["l0a_Wkr_swp"] = np.ascontiguousarray(Wkr[:, list(range(16, 32)) + list(range(16))])
    w["l0a_Wkvd"] = np.ascontiguousarray(d["L0_attn_Wkvd"])
    w["l0a_Wku"] = np.ascontiguousarray(d["L0_attn_Wku"])
    w["l0a_Wvu"] = np.ascontiguousarray(d["L0_attn_Wvu"])
    w["l0a_Wo"] = np.ascontiguousarray(d["L0_attn_Wo"])
    w["l0a_lobo"] = np.ascontiguousarray(d["L0_attn_lobo_log"].reshape(1, 16))
    return w


def declare_l0a_weights(P):
    W = {}
    for n, sh in [("l0a_Wq", [D, D]), ("l0a_Wqr_nat", [D, 512]), ("l0a_Wqr_swp", [D, 512]),
                  ("l0a_Wkr_nat", [D, 32]), ("l0a_Wkr_swp", [D, 32]), ("l0a_Wkvd", [D, 256]),
                  ("l0a_Wku", [256, D]), ("l0a_Wvu", [256, D]), ("l0a_Wo", [D, D]),
                  ("l0a_lobo", [1, 16])]:
        W[n] = P.dram(n, sh, F32, kind="ExternalInput")
    return W


def emit_l0_attn(P, W, C, xin, zout, stage=99, sub=9):
    SCALE = float(1.0 / np.sqrt(np.float32(96.0)))

    xbf = [P.sbuf(f"l0a_xbf{c}", [128, T], BF16) for c in range(NCH)]
    wqh = P.sbuf("l0a_wqh", [128, 2 * 1024], BF16)
    wqr = P.sbuf("l0a_wqr", [128, 8 * 512], BF16)
    wqrs = P.sbuf("l0a_wqrs", [128, 8 * 512], BF16)
    wkr = P.sbuf("l0a_wkr", [128, 8 * 32], BF16)
    wkrs = P.sbuf("l0a_wkrs", [128, 8 * 32], BF16)
    wkvd = P.sbuf("l0a_wkvd", [128, 8 * 256], BF16)
    wku = P.sbuf("l0a_wku", [128, 2 * 1024], BF16)
    wvu = P.sbuf("l0a_wvu", [128, 2 * 1024], BF16)
    wo = P.sbuf("l0a_wo", [128, 8 * 1024], BF16)
    lobo = P.sbuf("l0a_lobo_s", [1, 16], F32)
    loboC = P.sbuf("l0a_loboC", [1, 16], F32)
    loboCb = P.sbuf("l0a_loboCb", [128, 16], F32)

    latent = P.sbuf("l0a_latent", [128, 2 * T], BF16)
    qr_one = P.sbuf("l0a_qrone", [128, T], BF16)
    krt = P.sbuf("l0a_krt", [128, T], BF16)
    qtA = P.sbuf("l0a_qtA", [128, T], BF16)
    ktA = P.sbuf("l0a_ktA", [128, T], BF16)
    qtB = P.sbuf("l0a_qtB", [128, T], BF16)
    ktB = P.sbuf("l0a_ktB", [128, T], BF16)
    vtm = P.sbuf("l0a_vtm", [128, 16 * 128], BF16)
    vstage = P.sbuf("l0a_vstage", [128, 512], BF16)
    pst = [P.sbuf(f"l0a_p{i}", [128, T], BF16) for i in range(4)]
    ptsb = P.sbuf("l0a_ptsb", [128, 512], BF16)
    yrow = [P.sbuf(f"l0a_y{c}", [128, T], BF16) for c in range(NCH)]
    stash = P.sbuf("l0a_stash", [128, 512], F32)
    den = P.sbuf("l0a_den", [128, 16], F32)
    mx = P.sbuf("l0a_mx", [128, 16], F32)
    mfin = P.sbuf("l0a_mfin", [128, 4], F32)
    dfin = P.sbuf("l0a_dfin", [128, 4], F32)
    rt1 = P.sbuf("l0a_rt1", [128, 512], F32)
    rt2 = P.sbuf("l0a_rt2", [128, 512], F32)
    sink4 = P.sbuf("l0a_sink4", [128, 4], F32)
    xf = P.sbuf("l0a_xf", [128, 512], F32)
    zf = P.sbuf("l0a_zf", [128, 512], F32)

    ps_qk, ps_pr, ps_prB = P.psA, P.psB, P.psC
    ps_out, ps_wo, ps_pt = P.psD, P.psE, P.psT

    cosf, sinf, masks, identb = C["cosf"], C["sinf"], C["masks"], C["identb"]

    def loadw(dst, dram, nk, m):
        tk = None
        for k in range(nk):
            tk = P.op("g", lambda E, k=k: E.dma_start(dst[:, k * m:(k + 1) * m],
                                                      dram[k * 128:(k + 1) * 128, :]), inc="LG")
        return tk

    loadw(wqr, W["l0a_Wqr_nat"], 8, 512)
    loadw(wqrs, W["l0a_Wqr_swp"], 8, 512)
    loadw(wkr, W["l0a_Wkr_nat"], 8, 32)
    loadw(wkrs, W["l0a_Wkr_swp"], 8, 32)
    loadw(wkvd, W["l0a_Wkvd"], 8, 256)
    loadw(wku, W["l0a_Wku"], 2, 1024)
    loadw(wvu, W["l0a_Wvu"], 2, 1024)
    wtok = loadw(wo, W["l0a_Wo"], 8, 1024)
    tlob = P.op("y", lambda E: E.dma_start(lobo[:], W["l0a_lobo"][:]), inc="LY")
    ctok = P.op("s", lambda E: E.activation(loboC[:], lobo[:], AF.Exp), inc="S",
                waits=[tlob] + C["ready"])
    cbt = P.op("t", lambda E: E.matmul(ps_pr[:, 0:16], C["ones1x128"][0:1, :], loboC[0:1, :],
                                       start=True, stop=True), inc="T", waits=[ctok])
    cbv = P.op("v", lambda E: E.tensor_copy(loboCb[:], ps_pr[:, 0:16]), inc="V", waits=[cbt])

    for t_ in (qtA, ktA, qtB, ktB, krt):
        P.op("v", lambda E, t_=t_: E.memset(t_[:], 0.0))
    P.op("v", lambda E: E.memset(qr_one[:], 0.0), inc="V")

    if stage <= 1:
        return None
    last_store = None
    for r in range(NR):
        if stage <= 7 and r > 0:
            break
        xtok = None
        P.dbb("g", "LX")
        for ch in range(NCH):
            xtok = P.op("g", lambda E, ch=ch, r=r:
                        E.dma_start(xbf[ch][:], xin[r, ch * 128:(ch + 1) * 128, :]),
                        inc="LX", waits=[P.tok("T")] if ch == 0 else [])

        for mt in range(2):
            for tt_ in range(4):
                mm = None
                for ch in range(NCH):
                    mm = P.op("t", lambda E, mt=mt, tt_=tt_, ch=ch:
                              E.matmul(ps_pr[:], wkvd[:, ch * 256 + mt * 128:ch * 256 + mt * 128 + 128],
                                       xbf[ch][:, tt_ * 512:(tt_ + 1) * 512],
                                       start=(ch == 0), stop=(ch == NCH - 1)),
                              inc="T" if ch == NCH - 1 else None,
                              waits=(([xtok, wtok] if (mt == 0 and tt_ == 0) else []) +
                                     [P.tok("V"), P.tok("S")] if ch == 0 else []))
                P.op("v", lambda E, mt=mt, tt_=tt_:
                     E.tensor_copy(latent[:, mt * T + tt_ * 512:mt * T + (tt_ + 1) * 512],
                                   ps_pr[:]), inc="V", waits=[mm])

        for tt_ in range(4):
            sl = slice(tt_ * 512, (tt_ + 1) * 512)
            for ch in range(NCH):
                m1 = P.op("t", lambda E, tt_=tt_, ch=ch:
                          E.matmul(ps_pr[0:32, :], wkr[:, ch * 32:(ch + 1) * 32],
                                   xbf[ch][:, tt_ * 512:(tt_ + 1) * 512],
                                   start=(ch == 0), stop=(ch == NCH - 1)),
                          inc="T" if ch == NCH - 1 else None,
                          waits=[P.tok("V")] if ch == 0 else [])
            for ch in range(NCH):
                m2 = P.op("t", lambda E, tt_=tt_, ch=ch:
                          E.matmul(ps_prB[0:32, :], wkrs[:, ch * 32:(ch + 1) * 32],
                                   xbf[ch][:, tt_ * 512:(tt_ + 1) * 512],
                                   start=(ch == 0), stop=(ch == NCH - 1)),
                          inc="T" if ch == NCH - 1 else None,
                          waits=[P.tok("V")] if ch == 0 else [])
            P.op("v", lambda E, sl=sl: E.tensor_tensor(rt1[0:32, :], ps_pr[0:32, :],
                                                       cosf[0:32, sl], ALU.mult),
                 waits=[m1, m2])
            P.op("v", lambda E, sl=sl: E.tensor_tensor(rt2[0:32, :], ps_prB[0:32, :],
                                                       sinf[0:32, sl], ALU.mult))
            P.op("v", lambda E, sl=sl: E.tensor_tensor(krt[0:32, sl], rt1[0:32, :],
                                                       rt2[0:32, :], ALU.add), inc="V")
        if stage <= 2:
            return None
        krt_tok = P.tok("V")
        if r == 0:
            P.dbg("latent", latent[:, :], [128, 2 * T], BF16)
            P.dbg("krt", krt[0:32, :], [32, T], BF16)
        P.dbb("g", "LQ")
        krA = P.op("g", lambda E: E.dma_start(ktA[64:96, :], krt[0:32, :]), inc="LQ",
                   waits=[krt_tok, P.tok("T")])
        krB = P.op("v", lambda E: E.tensor_copy(ktB[0:32, :], krt[0:32, :]), inc="V",
                   waits=[krt_tok, P.tok("T")])

        if stage <= 3:
            return None
        qr_tok = None
        for hp in range(8):
            if stage <= 6 and hp > 0:
                break
            hA, hB = 2 * hp, 2 * hp + 1
            grp = hA // 4

            if hp % 2 == 0:
                for tt_ in range(4):
                    sl = slice(tt_ * 512, (tt_ + 1) * 512)
                    for ch in range(NCH):
                        m1 = P.op("t", lambda E, grp=grp, tt_=tt_, ch=ch:
                                  E.matmul(ps_pr[:], wqr[:, ch * 512 + grp * 128:ch * 512 + (grp + 1) * 128],
                                           xbf[ch][:, tt_ * 512:(tt_ + 1) * 512],
                                           start=(ch == 0), stop=(ch == NCH - 1)),
                                  inc="T" if ch == NCH - 1 else None,
                                  waits=[P.tok("V"), P.tok("S")] if ch == 0 else [])
                    for ch in range(NCH):
                        m2 = P.op("t", lambda E, grp=grp, tt_=tt_, ch=ch:
                                  E.matmul(ps_prB[:], wqrs[:, ch * 512 + grp * 128:ch * 512 + (grp + 1) * 128],
                                           xbf[ch][:, tt_ * 512:(tt_ + 1) * 512],
                                           start=(ch == 0), stop=(ch == NCH - 1)),
                                  inc="T" if ch == NCH - 1 else None,
                                  waits=[P.tok("V")] if ch == 0 else [])
                    P.op("v", lambda E, sl=sl: E.tensor_tensor(rt1[:], ps_pr[:], cosf[:, sl],
                                                               ALU.mult),
                         waits=[m1, m2, P.tok("LG")])
                    P.op("v", lambda E, sl=sl: E.tensor_tensor(rt2[:], ps_prB[:], sinf[:, sl],
                                                               ALU.mult))
                    P.op("v", lambda E, sl=sl: E.tensor_tensor(qr_one[:, sl], rt1[:], rt2[:],
                                                               ALU.add), inc="V")
                qr_tok = P.tok("V")

            wq_tok = None
            P.dbb("g", "LW")
            for k in range(NCH):
                wq_tok = P.op("g", lambda E, k=k, hp=hp:
                              E.dma_start(wqh[:, (hp % 2) * 1024 + k * 128:(hp % 2) * 1024 + (k + 1) * 128],
                                          W["l0a_Wq"][k * 128:(k + 1) * 128, hp * 128:(hp + 1) * 128]),
                              inc="LW", waits=[P.tok("T")] if k == 0 else [])

            for tt_ in range(4):
                sl = slice(tt_ * 512, (tt_ + 1) * 512)
                mm = None
                for kk in range(2):
                    mm = P.op("t", lambda E, hp=hp, tt_=tt_, kk=kk:
                              E.matmul(ps_pr[:], wku[:, kk * 1024 + hp * 128:kk * 1024 + (hp + 1) * 128],
                                       latent[:, kk * T + tt_ * 512:kk * T + (tt_ + 1) * 512],
                                       start=(kk == 0), stop=(kk == 1)),
                              inc="T" if kk == 1 else None,
                              waits=[P.tok("V"), P.tok("S")] if kk == 0 else [])
                P.op("v", lambda E, sl=sl: E.tensor_copy(ktA[0:64, sl], ps_pr[0:64, :]),
                     waits=[mm, krA])
                P.op("v", lambda E, sl=sl: E.tensor_copy(ktB[64:128, sl], ps_pr[64:128, :]),
                     inc="V", waits=[krB])
            kc_tok = P.tok("V")

            for tt_ in range(4):
                mm = None
                for kk in range(2):
                    mm = P.op("t", lambda E, hp=hp, tt_=tt_, kk=kk:
                              E.matmul(ps_prB[:], wvu[:, kk * 1024 + hp * 128:kk * 1024 + (hp + 1) * 128],
                                       latent[:, kk * T + tt_ * 512:kk * T + (tt_ + 1) * 512],
                                       start=(kk == 0), stop=(kk == 1)),
                              inc="T" if kk == 1 else None,
                              waits=[P.tok("V")] if kk == 0 else [])
                vs_ = P.op("v", lambda E: E.tensor_copy(vstage[:], ps_prB[:]), inc="V",
                           waits=[mm])
                mtr = None
                for cc in range(4):
                    mtr = P.op("t", lambda E, cc=cc:
                               E.transpose(ps_pt[:, cc * 128:(cc + 1) * 128],
                                           vstage[:, cc * 128:(cc + 1) * 128], identb[:]),
                               inc="T" if cc == 3 else None,
                               waits=[vs_, P.tok("V")] if cc == 0 else [])
                P.op("v", lambda E, tt_=tt_: E.tensor_copy(vtm[:, tt_ * 512:(tt_ + 1) * 512],
                                                           ps_pt[:]), inc="V", waits=[mtr])
            vtm_tok = P.tok("V")

            for par in range(2):
                h = hA if par == 0 else hB
                qt = qtA if par == 0 else qtB
                kt = ktA if par == 0 else ktB
                qsub = h % 4
                P.dbb("g", "LQ")
                if par == 0:
                    qr_dma = P.op("g", lambda E, qsub=qsub:
                                  E.dma_start(qtA[64:96, :], qr_one[qsub * 32:(qsub + 1) * 32, :]),
                                  inc="LQ", waits=[qr_tok, P.tok("T")])
                else:
                    qr_dma = P.op("g", lambda E, qsub=qsub:
                                  E.dma_start(qtB[0:32, :], qr_one[qsub * 32:(qsub + 1) * 32, :]),
                                  inc="LQ", waits=[qr_tok, P.tok("T")])
                qc_last = None
                for tt_ in range(4):
                    sl = slice(tt_ * 512, (tt_ + 1) * 512)
                    mm = None
                    for ch in range(NCH):
                        mm = P.op("t", lambda E, hp=hp, par=par, tt_=tt_, ch=ch:
                                  E.matmul(ps_pr[0:64, :],
                                           wqh[:, (hp % 2) * 1024 + ch * 128 + par * 64:(hp % 2) * 1024 + ch * 128 + par * 64 + 64],
                                           xbf[ch][:, tt_ * 512:(tt_ + 1) * 512],
                                           start=(ch == 0), stop=(ch == NCH - 1)),
                                  inc="T" if ch == NCH - 1 else None,
                                  waits=[wq_tok, P.tok("V")] if ch == 0 else [])
                    if par == 0:
                        qc_last = P.op("v", lambda E, sl=sl:
                                       E.tensor_copy(qtA[0:64, sl], ps_pr[0:64, :]),
                                       inc="V", waits=[mm])
                    else:
                        cpv = P.op("v", lambda E: E.tensor_copy(stash[0:64, :], ps_pr[0:64, :]),
                                   inc="V", waits=[mm, P.tok("LQ")])
                        P.dbb("g", "LQ")
                        qc_last = P.op("g", lambda E, sl=sl:
                                       E.dma_start(qtB[64:128, sl], stash[0:64, :]),
                                       inc="LQ", waits=[cpv])
                qk_ready = [qr_dma, qc_last, kc_tok, vtm_tok]
                if r == 0 and hp == 0 and par == 0:
                    P.dbg("qtA", qtA[:, :], [128, T], BF16)
                    P.dbg("ktA", ktA[:, :], [128, T], BF16)
                    P.dbg("vtm", vtm[:, :], [128, 16 * 128], BF16)
                if stage <= 4:
                    continue
                for g in range(4):
                    nblk = g + 1
                    acc0 = P.op("v", lambda E: E.memset(den[:], 0.0), inc="V",
                                waits=qk_ready if g == 0 else [])
                    for B in range(nblk):
                        diag = (B == g)
                        for i in range(4):
                            tq0 = g * 512 + i * 128
                            if diag:
                                P.op("t", lambda E, tq0=tq0, B=B, qt=qt, kt=kt:
                                     E.matmul(ps_qk[:], qt[:, tq0:tq0 + 128],
                                              kt[:, B * 512:(B + 1) * 512],
                                              start=True, stop=False),
                                     waits=[acc0, P.tok("S"), P.tok("V")])
                                mqk = P.op("t", lambda E, i=i:
                                           E.matmul(ps_qk[:], identb[:],
                                                    masks[:, i * 512:(i + 1) * 512],
                                                    start=False, stop=True), inc="T")
                            else:
                                mqk = P.op("t", lambda E, tq0=tq0, B=B, qt=qt, kt=kt:
                                           E.matmul(ps_qk[:], qt[:, tq0:tq0 + 128],
                                                    kt[:, B * 512:(B + 1) * 512],
                                                    start=True, stop=True), inc="T",
                                           waits=[acc0, P.tok("S"), P.tok("V")])
                            if sub >= 2:
                                P.op("v", lambda E, i=i, B=B:
                                     E.reduce_max(mx[:, i * 4 + B:i * 4 + B + 1], ps_qk[:], axis=AX.X),
                                     inc="V", waits=[mqk])
                            if sub >= 3:
                                etok = P.op("s", lambda E, i=i, B=B:
                                            E.activation(pst[i][:, B * 512:(B + 1) * 512], ps_qk[:],
                                                         AF.Exp, scale=SCALE), inc="S",
                                            waits=[mqk, P.tok("V")])
                            if sub >= 4:
                                P.op("v", lambda E, i=i, B=B:
                                     E.reduce_sum(den[:, i * 4 + B:i * 4 + B + 1],
                                                  pst[i][:, B * 512:(B + 1) * 512], axis=AX.X),
                                     inc="V", waits=[etok])
                    if stage <= 5:
                        continue
                    p1 = P.tok("V")
                    for i in range(4):
                        P.op("v", lambda E, i=i, nblk=nblk:
                             E.reduce_sum(dfin[:, i:i + 1], den[:, i * 4:i * 4 + nblk], axis=AX.X),
                             waits=[p1] if i == 0 else [])
                        P.op("v", lambda E, i=i, nblk=nblk:
                             E.reduce_max(mfin[:, i:i + 1], mx[:, i * 4:i * 4 + nblk], axis=AX.X),
                             inc="V")
                    P.wait("s", P.tok("V"))
                    P.sp("s", 2)
                    e2 = P.op("s", lambda E: E.activation(sink4[:], mfin[:], AF.Exp,
                                                          scale=SCALE), inc="S")
                    P.wait("v", e2)
                    P.wait("v", cbv)
                    P.sp("v", 2)
                    P.op("v", lambda E, h=h: E.tensor_tensor(sink4[:], sink4[:],
                                                             loboCb[:, h:h + 1].broadcast_to([128, 4]),
                                                             ALU.mult))
                    P.sp("v", 1)
                    P.op("v", lambda E: E.tensor_tensor(dfin[:], dfin[:], sink4[:], ALU.add))
                    P.sp("v", 1)
                    rv = P.op("v", lambda E: E.reciprocal(dfin[:], dfin[:]), inc="V")
                    P.sp("v", 2)
                    for i in range(4):
                        for B in range(nblk):
                            P.op("v", lambda E, i=i, B=B:
                                 E.tensor_tensor(pst[i][:, B * 512:(B + 1) * 512],
                                                 pst[i][:, B * 512:(B + 1) * 512],
                                                 dfin[:, i:i + 1].broadcast_to([128, 512]),
                                                 ALU.mult),
                                 inc="V", waits=[rv] if (i == 0 and B == 0) else [])
                    pn = P.tok("V")
                    if r == 0 and hp == 0 and par == 0 and g == 0:
                        P.dbg("pst0", pst[0][:, 0:512], [128, 512], BF16)
                        P.dbg("dfin", dfin[:, :], [128, 4])
                        P.dbg("mfin", mfin[:, :], [128, 4])
                    npv = nblk * 4
                    pv_i = 0
                    pv = None
                    for B in range(nblk):
                        for j in range(4):
                            mtr = None
                            for i in range(4):
                                mtr = P.op("t", lambda E, i=i, j=j, B=B:
                                           E.transpose(ps_pt[:, i * 128:(i + 1) * 128],
                                                       pst[i][:, B * 512 + j * 128:B * 512 + (j + 1) * 128],
                                                       identb[:]),
                                           inc="T" if i == 3 else None,
                                           waits=[pn, P.tok("V")] if (B == 0 and j == 0 and i == 0) else [])
                            ev = P.op("v", lambda E: E.tensor_copy(ptsb[:], ps_pt[:]), inc="V",
                                      waits=[mtr])
                            tkc = B * 4 + j
                            pv = P.op("t", lambda E, tkc=tkc, par=par, a=(pv_i == 0), z=(pv_i == npv - 1):
                                      E.matmul(ps_out[0:64, :],
                                               vtm[:, tkc * 128 + par * 64:tkc * 128 + par * 64 + 64],
                                               ptsb[:], start=a, stop=z),
                                      inc="T" if pv_i == npv - 1 else None, waits=[ev])
                            pv_i += 1
                    ytile = yrow[hp]
                    if par == 0:
                        P.op("v", lambda E, g=g, ytile=ytile:
                             E.tensor_copy(ytile[0:64, g * 512:(g + 1) * 512], ps_out[0:64, :]),
                             inc="V", waits=[pv])
                    else:
                        cpv = P.op("v", lambda E: E.tensor_copy(stash[0:64, :], ps_out[0:64, :]),
                                   inc="V", waits=[pv, P.tok("LG")])
                        P.op("g", lambda E, g=g, ytile=ytile:
                             E.dma_start(ytile[64:128, g * 512:(g + 1) * 512], stash[0:64, :]),
                             inc="LG", waits=[cpv])
        if r == 0:
            P.dbg("y0", yrow[0][:, :], [128, T], BF16)
            P.dbg("y7", yrow[7][:, :], [128, T], BF16)
        if stage <= 8:
            return None
        y_tok = [P.tok("V"), P.tok("LG")]

        for ch in range(NCH):
            for tt_ in range(4):
                ld = P.op("y", lambda E, r=r, ch=ch, tt_=tt_:
                          E.dma_start(xf[:], xin[r, ch * 128:(ch + 1) * 128, tt_ * 512:(tt_ + 1) * 512]),
                          inc="LY", waits=[P.tok("V")])
                mm = None
                for k in range(NCH):
                    mm = P.op("t", lambda E, ch=ch, k=k, tt_=tt_:
                              E.matmul(ps_wo[:], wo[:, k * 1024 + ch * 128:k * 1024 + (ch + 1) * 128],
                                       yrow[k][:, tt_ * 512:(tt_ + 1) * 512],
                                       start=(k == 0), stop=(k == NCH - 1)),
                              inc="T" if k == NCH - 1 else None,
                              waits=(y_tok + [P.tok("V")]) if k == 0 else [])
                a1 = P.op("v", lambda E: E.tensor_tensor(zf[:], ps_wo[:], xf[:], ALU.add),
                          inc="V", waits=[mm, ld, P.tok("ST")])
                last_store = P.op("y", lambda E, r=r, ch=ch, tt_=tt_:
                                  E.dma_start(zout[r, ch * 128:(ch + 1) * 128, tt_ * 512:(tt_ + 1) * 512],
                                              zf[:]), inc="ST", waits=[a1])
    return last_store





def prep_l0_ffn_weights(d):
    w = {}
    w["l0f_rt"] = np.ascontiguousarray(d["L0_ffn_routing_token"].reshape(D, 1))
    w["l0f_lg"] = np.ascontiguousarray(d["L0_ffn_light_gamma"].reshape(D, 1))
    w["l0f_hg"] = np.ascontiguousarray(d["L0_ffn_heavy_gamma"].reshape(D, 1))
    w["l0f_w1l"] = np.ascontiguousarray(d["L0_ffn_light_w1"])
    w["l0f_b1l"] = np.ascontiguousarray(d["L0_ffn_light_b1"].reshape(512, 1))
    w["l0f_w2l"] = np.ascontiguousarray(d["L0_ffn_light_w2"])
    w["l0f_b2l"] = np.ascontiguousarray(d["L0_ffn_light_b2"].reshape(D, 1))
    w["l0f_w1h"] = np.ascontiguousarray(d["L0_ffn_heavy_w1"])
    w["l0f_b1h"] = np.ascontiguousarray(d["L0_ffn_heavy_b1"].reshape(4096, 1))
    w["l0f_w2h"] = np.ascontiguousarray(d["L0_ffn_heavy_w2"])
    w["l0f_b2h"] = np.ascontiguousarray(d["L0_ffn_heavy_b2"].reshape(D, 1))
    return w


def declare_l0f_weights(P):
    W = {}
    for n, sh in [("l0f_rt", [D, 1]), ("l0f_lg", [D, 1]), ("l0f_hg", [D, 1]),
                  ("l0f_w1l", [D, 512]), ("l0f_b1l", [512, 1]), ("l0f_w2l", [512, D]),
                  ("l0f_b2l", [D, 1]), ("l0f_w1h", [D, 4096]), ("l0f_b1h", [4096, 1]),
                  ("l0f_w2h", [4096, D]), ("l0f_b2h", [D, 1])]:
        W[n] = P.dram(n, sh, F32, kind="ExternalInput")
    return W


def _loadvec(P, dst, col0, dram, nch, q="y"):
    tk = None
    for c in range(nch):
        tk = P.op(q, lambda E, c=c: E.dma_start(dst[:, col0 + c:col0 + c + 1],
                                                dram[c * 128:(c + 1) * 128, :]),
                  inc="LY" if q == "y" else "LG")
    return tk


def emit_l0_ffn(P, W, C, xin, zout, hscr):
    NH = 4096
    LOGK = float(np.log(np.float32(1152.0)))
    A1 = float(np.float32(LOGK) - np.log(np.float32(2048.0)))

    ones1x = C["ones1x128"]
    onesb = C["ones128_bf"]
    identf = C["identf"]

    vecs = P.sbuf("l0f_vecs", [128, 8 * 4], F32)
    b1l_s = P.sbuf("l0f_b1ls_", [128, 4], F32)
    b1h_s = P.sbuf("l0f_b1hs_", [128, 32], F32)
    b2h_s = P.sbuf("l0f_b2hs_", [128, 8], F32)
    hlight = [P.sbuf(f"l0f_hl{k}", [128, TT], BF16) for k in range(4)]
    maskb = P.sbuf("l0f_maskb", [128, TT], F32)
    s2 = P.sbuf("l0f_s2", [2, T], F32)

    t_rt = _loadvec(P, vecs, 0, W["l0f_rt"], 8)
    _loadvec(P, vecs, 8, W["l0f_lg"], 8)
    _loadvec(P, vecs, 16, W["l0f_hg"], 8)
    _loadvec(P, vecs, 24, W["l0f_b2l"], 8)
    _loadvec(P, b1l_s, 0, W["l0f_b1l"], 4)
    _loadvec(P, b1h_s, 0, W["l0f_b1h"], 32)
    tvec = _loadvec(P, b2h_s, 0, W["l0f_b2h"], 8)

    ps1, ps2, psBC, psTR, psMM = P.psA, P.psB, P.psC, P.psD, P.psE

    outer_scope = P.scope()
    outer_scope.__enter__()
    xrms = [P.sbuf(f"l0f_xr{c}", [128, TT], BF16) for c in range(NCH)]

    with P.scope():
        rnorm = P.sbuf("l0f_rnorm", [1, TT], F32)
        sumsq = P.sbuf("l0f_sumsq", [1, TT], F32)
        stmp = P.sbuf("l0f_stmp", [1, TT], F32)
        xf = P.sbuf("l0f_xf", [128, 512], F32)
        x2bf = P.sbuf("l0f_x2bf", [128, 512], BF16)
        prod = P.sbuf("l0f_prod", [128, 512], F32)
        prodh = P.sbuf("l0f_prodh", [128, 512], BF16)
        prodl = P.sbuf("l0f_prodl", [128, 512], BF16)
        prodm = P.sbuf("l0f_prodm", [128, 512], BF16)
        sstg = P.sbuf("l0f_sstg", [1, 512], F32)

        P.barrier()
        ntile = TT // 512
        for it in range(ntile):
            r, tl = divmod(it, T // 512)
            tl *= 512
            mm = None
            for ch in range(NCH):
                ld = P.op("y", lambda E, r=r, ch=ch, tl=tl:
                          E.dma_start(xf[:], xin[r, ch * 128:(ch + 1) * 128, tl:tl + 512]),
                          inc="LY", waits=[P.tok("V"), P.tok("S"), P.tok("T")])
                cs = P.op("s", lambda E: E.activation(x2bf[:], xf[:], AF.Square), inc="S",
                          waits=[ld])
                mm = P.op("t", lambda E, ch=ch: E.matmul(ps1[0:1, :], onesb[:, 0:1], x2bf[:],
                                                         start=(ch == 0), stop=(ch == NCH - 1)),
                          inc="T", waits=[cs, P.tok("V")] + (C["ready"] if it == 0 and ch == 0 else []))
                P.op("v", lambda E, ch=ch: E.tensor_scalar(prod[:], xf[:], vecs[:, ch:ch + 1],
                                                           None, ALU.mult),
                     waits=[ld, t_rt, P.tok("T")])
                P.op("v", lambda E: E.tensor_copy(prodh[:], prod[:]))
                P.op("v", lambda E: E.tensor_tensor(prod[:], prod[:], prodh[:], ALU.subtract))
                P.op("v", lambda E: E.tensor_copy(prodl[:], prod[:]))
                P.op("v", lambda E: E.tensor_tensor(prod[:], prod[:], prodl[:], ALU.subtract))
                pv = P.op("v", lambda E: E.tensor_copy(prodm[:], prod[:]), inc="V")
                for pt, tile_ in enumerate((prodh, prodl, prodm)):
                    P.op("t", lambda E, ch=ch, pt=pt, tile_=tile_:
                         E.matmul(ps2[0:1, :], onesb[:, 0:1], tile_[:],
                                  start=(ch == 0 and pt == 0), stop=(ch == NCH - 1 and pt == 2)),
                         inc="T" if pt == 2 else None,
                         waits=[pv] if pt == 0 else [])
            smm = P.tok("T")
            if r == 0:
                P.op("v", lambda E, tl=tl: E.tensor_copy(s2[0:1, tl:tl + 512], ps2[0:1, :]),
                     inc="V", waits=[smm])
            else:
                sv = P.op("v", lambda E: E.tensor_copy(sstg[:], ps2[0:1, :]), inc="V",
                          waits=[smm])
                P.dbb("g", "LQ")
                P.op("g", lambda E, tl=tl: E.dma_start(s2[1:2, tl:tl + 512], sstg[:]),
                     inc="LQ", waits=[sv])
            P.op("v", lambda E, it=it: E.tensor_copy(sumsq[:, it * 512:(it + 1) * 512], ps1[0:1, :]),
                 inc="V", waits=[mm])
        t1 = P.op("s", lambda E: E.activation(stmp[:], sumsq[:], AF.Sqrt), inc="S",
                  waits=[P.tok("V")])
        t2 = P.op("v", lambda E: E.tensor_scalar(stmp[:], stmp[:], 1e-12, None, ALU.max),
                  inc="V", waits=[t1])
        t3 = P.op("v", lambda E: E.reciprocal(rnorm[:], stmp[:]), inc="V")
        t4 = P.op("v", lambda E: E.tensor_scalar(rnorm[:], rnorm[:], float(np.sqrt(1024.0)),
                                                 None, ALU.mult), inc="V")

        for it in range(ntile):
            r, tl = divmod(it, T // 512)
            tl *= 512
            b1 = P.op("t", lambda E, it=it: E.matmul(psBC[:], ones1x[0:1, :],
                                                     rnorm[0:1, it * 512:(it + 1) * 512],
                                                     start=True, stop=True), inc="T",
                      waits=[t4, P.tok("V")])
            for ch in range(NCH):
                ld = P.op("y", lambda E, r=r, ch=ch, tl=tl:
                          E.dma_start(xf[:], xin[r, ch * 128:(ch + 1) * 128, tl:tl + 512]),
                          inc="LY", waits=[P.tok("V")])
                P.op("v", lambda E, ch=ch, it=it:
                     E.tensor_tensor(xrms[ch][:, it * 512:(it + 1) * 512], xf[:], psBC[:],
                                     ALU.mult), inc="V", waits=[ld, b1])
        xr_tok = P.tok("V")
        P.dbg("rnorm", rnorm[0:1, :], [1, TT])
        P.dbg("xr0", xrms[0][:, :], [128, TT], BF16)

    with P.scope():
        nega = P.sbuf("l0f_nega", [2, 1], F32)
        msh = P.sbuf("l0f_msh", [2, 1], F32)
        nmsh = P.sbuf("l0f_nmsh", [2, 1], F32)
        maxs = P.sbuf("l0f_maxs", [2, 1], F32)
        ssum = P.sbuf("l0f_ssum", [2, 1], F32)
        lns = P.sbuf("l0f_lns", [2, 1], F32)
        tn1 = P.sbuf("l0f_tn1", [2, 1], F32)
        clip2 = P.sbuf("l0f_clip", [2, T], F32)
        etile = P.sbuf("l0f_et", [2, T], F32)
        lo2 = P.sbuf("l0f_lo", [2, 1], F32)
        hi2 = P.sbuf("l0f_hi", [2, 1], F32)
        mid2 = P.sbuf("l0f_mid", [2, 1], F32)
        take2 = P.sbuf("l0f_take", [2, 1], F32)
        ntk2 = P.sbuf("l0f_ntk", [2, 1], F32)
        cnt2 = P.sbuf("l0f_cnt", [2, 1], F32)
        junk2 = None
        vk2 = P.sbuf("l0f_vk", [2, 1], F32)
        tie2 = P.sbuf("l0f_tie", [2, T], F32)
        pref2 = P.sbuf("l0f_pref", [2, T], F32)
        mask2 = P.sbuf("l0f_mask2", [2, T], F32)
        maskr1 = P.sbuf("l0f_maskr1", [1, T], F32)
        defc = P.sbuf("l0f_defc", [2, 1], F32)

        P.barrier()
        P.op("v", lambda E: E.memset(nega[:], -A1), inc="V")
        P.op("v", lambda E: E.reduce_max(maxs[:], s2[:], axis=AX.X), inc="V")
        P.sp("v", 2)
        for it in range(49):
            P.op("v", lambda E: E.tensor_tensor(msh[:], maxs[:], nega[:], ALU.min))
            P.op("v", lambda E: E.tensor_tensor(clip2[:], s2[:],
                                                nega[:, 0:1].broadcast_to([2, T]), ALU.min))
            P.op("v", lambda E: E.tensor_tensor(etile[:], clip2[:],
                                                msh[:, 0:1].broadcast_to([2, T]),
                                                ALU.subtract), inc="V")
            P.wait("s", P.tok("V"))
            e1 = P.op("s", lambda E: E.activation(etile[:], etile[:], AF.Exp), inc="S")
            sm = P.op("v", lambda E: E.reduce_sum(ssum[:], etile[:], axis=AX.X), inc="V",
                      waits=[e1])
            P.wait("s", sm)
            P.sp("s", 2)
            l1 = P.op("s", lambda E: E.activation(lns[:], ssum[:], AF.Ln), inc="S")
            P.sp("s", 2)
            l2 = P.op("s", lambda E: E.activation(tn1[:], lns[:], AF.Exp, scale=-1.0), inc="S")
            P.wait("v", l2)
            P.sp("v", 2)
            P.op("v", lambda E: E.tensor_tensor(tn1[:], tn1[:], ssum[:], ALU.mult))
            P.sp("v", 1)
            P.op("v", lambda E: E.tensor_scalar(tn1[:], tn1[:], -1.0, None, ALU.add))
            P.sp("v", 1)
            P.op("v", lambda E: E.tensor_tensor(tn1[:], lns[:], tn1[:], ALU.add))
            P.sp("v", 1)
            P.op("v", lambda E: E.tensor_tensor(nega[:], msh[:], tn1[:], ALU.add))
            P.sp("v", 1)
            P.op("v", lambda E: E.tensor_scalar(nega[:], nega[:], -LOGK, None, ALU.add),
                 inc="V")
            P.sp("v", 2)
        cl = P.op("v", lambda E: E.tensor_tensor(clip2[:], s2[:],
                                                 nega[:, 0:1].broadcast_to([2, T]), ALU.min),
                  inc="V")
        P.op("v", lambda E: E.tensor_scalar(lo2[:], maxs[:], -300.0, None, ALU.add))
        P.op("v", lambda E: E.tensor_scalar(hi2[:], maxs[:], 1.0, None, ALU.add), inc="V")
        P.sp("v", 2)
        for it in range(40):
            P.op("v", lambda E: E.tensor_tensor(mid2[:], lo2[:], hi2[:], ALU.add))
            P.sp("v", 1)
            P.op("v", lambda E: E.tensor_scalar(mid2[:], mid2[:], 0.5, None, ALU.mult))
            P.sp("v", 1)
            P.op("v", lambda E: E.tensor_tensor(pref2[:], clip2[:],
                                                mid2[:, 0:1].broadcast_to([2, T]), ALU.is_gt))
            P.op("v", lambda E: E.reduce_sum(cnt2[:], pref2[:], axis=AX.X))
            P.sp("v", 2)
            P.op("v", lambda E: E.tensor_scalar(take2[:], cnt2[:], 1024.0, None, ALU.is_ge))
            P.sp("v", 1)
            P.op("v", lambda E: E.tensor_scalar(ntk2[:], take2[:], -1.0, 1.0, ALU.mult, ALU.add))
            P.sp("v", 1)
            P.op("v", lambda E: E.tensor_tensor(tn1[:], take2[:], mid2[:], ALU.mult))
            P.op("v", lambda E: E.tensor_tensor(lo2[:], ntk2[:], lo2[:], ALU.mult))
            P.sp("v", 1)
            P.op("v", lambda E: E.tensor_tensor(lo2[:], lo2[:], tn1[:], ALU.add))
            P.op("v", lambda E: E.tensor_tensor(tn1[:], take2[:], hi2[:], ALU.mult))
            P.op("v", lambda E: E.tensor_tensor(hi2[:], ntk2[:], mid2[:], ALU.mult))
            P.sp("v", 1)
            P.op("v", lambda E: E.tensor_tensor(hi2[:], hi2[:], tn1[:], ALU.add), inc="V")
            P.sp("v", 2)
        P.op("v", lambda E: E.tensor_tensor(etile[:], clip2[:],
                                            hi2[:, 0:1].broadcast_to([2, T]), ALU.is_gt))
        P.op("v", lambda E: E.tensor_scalar(etile[:], etile[:], -1e38, None, ALU.mult))
        P.op("v", lambda E: E.tensor_tensor(etile[:], etile[:], clip2[:], ALU.add))
        P.op("v", lambda E: E.reduce_max(vk2[:], etile[:], axis=AX.X), inc="V")
        P.sp("v", 2)
        P.op("v", lambda E: E.tensor_tensor(mask2[:], clip2[:],
                                            vk2[:, 0:1].broadcast_to([2, T]), ALU.is_gt))
        P.op("v", lambda E: E.reduce_sum(cnt2[:], mask2[:], axis=AX.X))
        P.sp("v", 2)
        P.op("v", lambda E: E.tensor_scalar(defc[:], cnt2[:], -1.0, 1024.0, ALU.mult, ALU.add))
        P.op("v", lambda E: E.tensor_tensor(tie2[:], clip2[:],
                                            vk2[:, 0:1].broadcast_to([2, T]), ALU.is_equal))
        P.op("v", lambda E: E.memset(etile[:], 0.0), inc="V")
        P.sp("v", 1)
        P.op("v", lambda E: E.tensor_tensor_scan(pref2[:], tie2[:], etile[:], 0.0,
                                                 ALU.add, ALU.add))
        P.op("v", lambda E: E.tensor_tensor(pref2[:], pref2[:],
                                            defc[:, 0:1].broadcast_to([2, T]), ALU.is_le))
        P.op("v", lambda E: E.tensor_tensor(tie2[:], tie2[:], pref2[:], ALU.mult))
        msk = P.op("v", lambda E: E.tensor_tensor(mask2[:], mask2[:], tie2[:], ALU.add),
                   inc="V")
        P.dbb("g", "LQ")
        dmr = P.op("g", lambda E: E.dma_start(maskr1[:], mask2[1:2, :]), inc="LQ", waits=[msk])
        for it in range(TT // 512):
            r, tl = divmod(it, T // 512)
            tl *= 512
            src = (lambda E, tl=tl: E.matmul(psBC[:], ones1x[0:1, :], mask2[0:1, tl:tl + 512],
                                             start=True, stop=True)) if r == 0 else                   (lambda E, tl=tl: E.matmul(psBC[:], ones1x[0:1, :], maskr1[0:1, tl:tl + 512],
                                             start=True, stop=True))
            bmm = P.op("t", lambda E, src=src: src(E), inc="T",
                       waits=[msk, dmr, P.tok("V")])
            P.op("v", lambda E, it=it: E.tensor_copy(maskb[:, it * 512:(it + 1) * 512], psBC[:]),
                 inc="V", waits=[bmm])
        mask_tok = P.tok("V")
        P.dbg("s2", s2[:, :], [2, T])
        P.dbg("clip2", clip2[:, :], [2, T])
        P.dbg("mask2", mask2[:, :], [2, T])
        P.dbg("maskb", maskb[:, :], [128, TT])

    with P.scope():
        w1l_s = P.sbuf("l0f_w1ls", [128, 8 * 512], BF16)
        w1h_c = P.sbuf("l0f_w1hc", [128, 2 * 8 * 128], BF16)
        hstg = P.sbuf("l0f_hstg", [128, 512], BF16)

        P.barrier()
        tk = None
        for k in range(NCH):
            tk = P.op("g", lambda E, k=k: E.dma_start(w1l_s[:, k * 512:(k + 1) * 512],
                                                      W["l0f_w1l"][k * 128:(k + 1) * 128, :]),
                      inc="LG")
        sc = None
        for k in range(NCH):
            sc = P.op("v", lambda E, k=k: E.tensor_scalar(w1l_s[:, k * 512:(k + 1) * 512],
                                                          w1l_s[:, k * 512:(k + 1) * 512],
                                                          vecs[:, 8 + k:9 + k], None, ALU.mult),
                      inc="V", waits=[tk, tvec])
        for mt in range(4):
            for it in range(TT // 512):
                mm = None
                for k in range(NCH):
                    mm = P.op("t", lambda E, mt=mt, it=it, k=k:
                              E.matmul(psMM[:], w1l_s[:, k * 512 + mt * 128:k * 512 + (mt + 1) * 128],
                                       xrms[k][:, it * 512:(it + 1) * 512],
                                       start=(k == 0), stop=(k == NCH - 1)),
                              inc="T" if k == NCH - 1 else None,
                              waits=[sc, xr_tok, P.tok("S")] if k == 0 else [])
                P.op("s", lambda E, mt=mt, it=it:
                     E.activation(hlight[mt][:, it * 512:(it + 1) * 512], psMM[:], AF.Gelu,
                                  bias=b1l_s[:, mt:mt + 1]), inc="S", waits=[mm])
        hl_tok = P.tok("S")
        P.dbg("hl0", hlight[0][:, :], [128, TT], BF16)

        for hk in range(32):
            buf = hk % 2
            tk = None
            P.dbb("g", "LW")
            for k in range(NCH):
                tk = P.op("g", lambda E, hk=hk, k=k, buf=buf:
                          E.dma_start(w1h_c[:, buf * 1024 + k * 128:buf * 1024 + (k + 1) * 128],
                                      W["l0f_w1h"][k * 128:(k + 1) * 128, hk * 128:(hk + 1) * 128]),
                          inc="LW", waits=[P.tok("T")] if k == 0 else [])
            sc = None
            for k in range(NCH):
                sc = P.op("v", lambda E, k=k, buf=buf:
                          E.tensor_scalar(w1h_c[:, buf * 1024 + k * 128:buf * 1024 + (k + 1) * 128],
                                          w1h_c[:, buf * 1024 + k * 128:buf * 1024 + (k + 1) * 128],
                                          vecs[:, 16 + k:17 + k], None, ALU.mult),
                          inc="V", waits=[tk] if k == 0 else [])
            for it in range(TT // 512):
                mm = None
                for k in range(NCH):
                    mm = P.op("t", lambda E, it=it, k=k, buf=buf:
                              E.matmul(psMM[:], w1h_c[:, buf * 1024 + k * 128:buf * 1024 + (k + 1) * 128],
                                       xrms[k][:, it * 512:(it + 1) * 512],
                                       start=(k == 0), stop=(k == NCH - 1)),
                              inc="T" if k == NCH - 1 else None,
                              waits=[sc, P.tok("S")] if k == 0 else [])
                ac = P.op("s", lambda E, hk=hk: E.activation(hstg[:], psMM[:], AF.Gelu,
                                                             bias=b1h_s[:, hk:hk + 1]),
                          inc="S", waits=[mm, P.tok("SH")])
                P.dbb("g", "SH")
                P.op("g", lambda E, hk=hk, it=it:
                     E.dma_start(hscr[hk * 128:(hk + 1) * 128, it * 512:(it + 1) * 512], hstg[:]),
                     inc="SH", waits=[ac])

    outer_scope.__exit__(None, None, None)

    with P.scope():
        w2l_s = P.sbuf("l0f_w2ls", [128, 4 * 1024], BF16)
        w2h_s = P.sbuf("l0f_w2hs", [128, 32 * 1024], BF16)
        hmk = [P.sbuf(f"l0f_hmk{k}", [128, 512], BF16) for k in range(32)]
        xf2 = P.sbuf("l0f_xf2", [128, 512], F32)
        zt = P.sbuf("l0f_zt", [128, 512], F32)
        tb = P.sbuf("l0f_tb", [128, 512], F32)

        P.barrier()
        tk = None
        for k in range(4):
            tk = P.op("g", lambda E, k=k: E.dma_start(w2l_s[:, k * 1024:(k + 1) * 1024],
                                                      W["l0f_w2l"][k * 128:(k + 1) * 128, :]),
                      inc="LG")
        for k in range(32):
            tk = P.op("g", lambda E, k=k: E.dma_start(w2h_s[:, k * 1024:(k + 1) * 1024],
                                                      W["l0f_w2h"][k * 128:(k + 1) * 128, :]),
                      inc="LG")
        w2tok = tk
        for it in range(TT // 512):
            r, tl = divmod(it, T // 512)
            tl *= 512
            hm_last = None
            P.dbb("y", "LH")
            for k in range(32):
                P.op("y", lambda E, k=k, it=it:
                     E.dma_start(hmk[k][:], hscr[k * 128:(k + 1) * 128, it * 512:(it + 1) * 512]),
                     inc="LH", waits=[P.tok("T")] if k == 0 else [])
            ld_all = P.tok("LH")
            for k in range(32):
                hm_last = P.op("v", lambda E, k=k, it=it:
                               E.tensor_tensor(hmk[k][:], hmk[k][:],
                                               maskb[:, it * 512:(it + 1) * 512], ALU.mult),
                               inc="V", waits=[ld_all] + ([mask_tok] if k == 0 else []))
            for ch in range(NCH):
                mm = None
                for k in range(4):
                    mm = P.op("t", lambda E, ch=ch, k=k, it=it:
                              E.matmul(psMM[:], w2l_s[:, k * 1024 + ch * 128:k * 1024 + (ch + 1) * 128],
                                       hlight[k][:, it * 512:(it + 1) * 512],
                                       start=(k == 0), stop=False),
                              waits=[w2tok, hm_last, hl_tok, P.tok("V"), P.tok("S")] if k == 0 else [])
                for k in range(32):
                    mm = P.op("t", lambda E, ch=ch, k=k:
                              E.matmul(psMM[:], w2h_s[:, k * 1024 + ch * 128:k * 1024 + (ch + 1) * 128],
                                       hmk[k][:], start=False, stop=(k == 31)),
                              inc="T" if k == 31 else None)
                ld = P.op("y", lambda E, r=r, ch=ch, tl=tl:
                          E.dma_start(xf2[:], xin[r, ch * 128:(ch + 1) * 128, tl:tl + 512]),
                          inc="LY", waits=[P.tok("V")])
                a1 = P.op("v", lambda E: E.tensor_tensor(zt[:], psMM[:], xf2[:], ALU.add),
                          waits=[mm, ld, P.tok("ST")])
                a2 = P.op("v", lambda E, ch=ch: E.tensor_scalar(zt[:], zt[:],
                                                                vecs[:, 24 + ch:25 + ch], None,
                                                                ALU.add), inc="V")
                a3 = P.op("v", lambda E, ch=ch, it=it:
                          E.scalar_tensor_tensor(zt[:], maskb[:, it * 512:(it + 1) * 512],
                                                 b2h_s[:, ch:ch + 1], zt[:],
                                                 ALU.mult, ALU.add), inc="V")
                P.op("y", lambda E, r=r, ch=ch, tl=tl:
                     E.dma_start(zout[r, ch * 128:(ch + 1) * 128, tl:tl + 512], zt[:]),
                     inc="ST", waits=[a3])
        P.dbg("z0", None, [128, TT], dram_src=(zout, 0))
        P.dbg("hl1", hlight[1][:, :], [128, TT], BF16)
    return None






def prep_l1_attn_weights(d):
    w = {}
    for n, key in [("l1a_ln0g", "L1_attn_ln0_g"), ("l1a_ln0b", "L1_attn_ln0_b"),
                   ("l1a_ln1g", "L1_attn_ln1_g"), ("l1a_ln1b", "L1_attn_ln1_b"),
                   ("l1a_td", "L1_attn_time_decay"), ("l1a_tf", "L1_attn_time_first"),
                   ("l1a_tmk", "L1_attn_tm_k"), ("l1a_tmv", "L1_attn_tm_v"),
                   ("l1a_tmr", "L1_attn_tm_r"), ("l1a_cmk", "L1_attn_cm_k"),
                   ("l1a_cmr", "L1_attn_cm_r")]:
        w[n] = np.ascontiguousarray(np.asarray(d[key], np.float32).reshape(D, 1))
    for n, key in [("l1a_Wk", "L1_attn_Wk"), ("l1a_Wv", "L1_attn_Wv"),
                   ("l1a_Wr", "L1_attn_Wr"), ("l1a_Wo", "L1_attn_Wo"),
                   ("l1a_Wck", "L1_attn_Wck"), ("l1a_Wcv", "L1_attn_Wcv"),
                   ("l1a_Wcr", "L1_attn_Wcr")]:
        w[n] = np.ascontiguousarray(np.asarray(d[key], np.float32))
    return w


def declare_l1a_weights(P):
    W = {}
    for n in ("l1a_ln0g", "l1a_ln0b", "l1a_ln1g", "l1a_ln1b", "l1a_td", "l1a_tf",
              "l1a_tmk", "l1a_tmv", "l1a_tmr", "l1a_cmk", "l1a_cmr"):
        W[n] = P.dram(n, [D, 1], F32, kind="ExternalInput")
    for n, sh in [("l1a_Wk", [D, D]), ("l1a_Wv", [D, D]), ("l1a_Wr", [D, D]),
                  ("l1a_Wo", [D, D]), ("l1a_Wck", [D, 4096]), ("l1a_Wcv", [4096, D]),
                  ("l1a_Wcr", [D, D])]:
        W[n] = P.dram(n, sh, F32, kind="ExternalInput")
    return W


def emit_lnstats_dram(P, C, src, r, mean, rstd, pfx):
    onesb = C["ones128_bf"]
    ps1, ps2 = P.psA, P.psB
    with P.scope():
        zf = P.sbuf(pfx + "_zf", [128, 512], F32)
        zbf = P.sbuf(pfx + "_zbf", [128, 512], BF16)
        z2bf = P.sbuf(pfx + "_z2bf", [128, 512], BF16)
        tmp1 = P.sbuf(pfx + "_tmp1", [1, T], F32)
        for it in range(T // 512):
            tl = it * 512
            mm = None
            for ch in range(NCH):
                ld = P.op("y", lambda E, r=r, ch=ch, tl=tl:
                          E.dma_start(zf[:], src[r, ch * 128:(ch + 1) * 128, tl:tl + 512]),
                          inc="LY", waits=[P.tok("V"), P.tok("S"), P.tok("T")])
                cv = P.op("v", lambda E: E.tensor_copy(zbf[:], zf[:]), inc="V",
                          waits=[ld, P.tok("T")])
                cs = P.op("s", lambda E: E.activation(z2bf[:], zf[:], AF.Square), inc="S",
                          waits=[ld, P.tok("T")])
                P.op("t", lambda E, ch=ch: E.matmul(ps1[0:1, :], onesb[:, 0:1], zbf[:],
                                                    start=(ch == 0), stop=(ch == NCH - 1)),
                     waits=[cv])
                mm = P.op("t", lambda E, ch=ch: E.matmul(ps2[0:1, :], onesb[:, 0:1], z2bf[:],
                                                         start=(ch == 0), stop=(ch == NCH - 1)),
                          inc="T", waits=[cs])
            P.op("v", lambda E, tl=tl: E.tensor_copy(mean[:, tl:tl + 512], ps1[0:1, :]),
                 waits=[mm])
            P.op("v", lambda E, tl=tl: E.tensor_copy(rstd[:, tl:tl + 512], ps2[0:1, :]),
                 inc="V")
        t1 = P.op("v", lambda E: E.tensor_scalar(mean[:], mean[:], 1.0 / D, None, ALU.mult),
                  inc="V")
        t2 = P.op("s", lambda E: E.activation(tmp1[:], mean[:], AF.Square), inc="S",
                  waits=[t1])
        P.op("v", lambda E: E.scalar_tensor_tensor(tmp1[:], rstd[:], 1.0 / D, tmp1[:],
                                                   ALU.mult, ALU.subtract), waits=[t2])
        t4 = P.op("v", lambda E: E.tensor_scalar(tmp1[:], tmp1[:], 1e-5, None, ALU.add),
                  inc="V")
        t5 = P.op("s", lambda E: E.activation(tmp1[:], tmp1[:], AF.Sqrt), inc="S", waits=[t4])
        P.op("v", lambda E: E.reciprocal(rstd[:], tmp1[:]), inc="V", waits=[t5])
        P.sp("v", 1)


def emit_l1_attn(P, W, C, xin, zout, hscr, xiscr):
    ones1x = C["ones1x128"]
    psBC, psB2, psE, psF = P.psC, P.psD, P.psE, P.psF

    vec = P.sbuf("l1a_vec", [128, 96], F32)
    for i, n in enumerate(("l1a_ln0g", "l1a_ln0b", "l1a_ln1g", "l1a_ln1b", "l1a_tmk",
                           "l1a_tmv", "l1a_tmr", "l1a_cmk", "l1a_cmr", "l1a_tf")):
        _loadvec(P, vec, 8 * i, W[n], 8)
    tdcol = P.sbuf("l1a_tdc", [128, 8], F32)
    vtok = _loadvec(P, tdcol, 0, W["l1a_td"], 8)
    P.op("s", lambda E: E.activation(tdcol[:], tdcol[:], AF.Exp), inc="S",
         waits=[vtok] + C["ready"])
    P.sp("s", 2)
    P.op("s", lambda E: E.activation(vec[:, 80:88], tdcol[:], AF.Exp, scale=-1.0), inc="S")
    VC = {"ln0g": 0, "ln0b": 8, "ln1g": 16, "ln1b": 24, "tmk": 32, "tmv": 40, "tmr": 48,
          "cmk": 56, "cmr": 64, "u": 72, "ew": 80}

    mean0 = P.sbuf("l1a_mean0", [1, T], F32)
    rstd0 = P.sbuf("l1a_rstd0", [1, T], F32)
    mean1, rstd1 = mean0, rstd0

    def affine_mix_pass(src, r, mean, rstd, gc, bc, specs, pfx):
        with P.scope():
            xrow = P.sbuf(pfx + "_xrow", [128, T], F32)
            srow = P.sbuf(pfx + "_srow", [128, T], F32)
            for ch in range(NCH):
                ld = P.op("g", lambda E, r=r, ch=ch:
                          E.dma_start(srow[:], src[r, ch * 128:(ch + 1) * 128, :]),
                          inc="LX",
                          waits=[P.tok("V"), P.tok("S"), P.tok("T")] if ch == 0
                          else [P.tok("V")])
                for tt_ in range(4):
                    sl = slice(tt_ * 512, (tt_ + 1) * 512)
                    P.op("t", lambda E, sl=sl: E.matmul(psBC[:], ones1x[0:1, :],
                                                        mean[0:1, sl], start=True, stop=True),
                         waits=[P.tok("V")])
                    b2 = P.op("t", lambda E, sl=sl: E.matmul(psB2[:], ones1x[0:1, :],
                                                             rstd[0:1, sl], start=True,
                                                             stop=True), inc="T")
                    P.op("v", lambda E, sl=sl: E.tensor_tensor(xrow[:, sl], srow[:, sl],
                                                               psBC[:], ALU.subtract),
                         waits=[ld, b2])
                    P.op("v", lambda E, sl=sl: E.tensor_tensor(xrow[:, sl], xrow[:, sl],
                                                               psB2[:], ALU.mult))
                    P.op("v", lambda E, sl=sl, ch=ch:
                         E.tensor_scalar(xrow[:, sl], xrow[:, sl],
                                         vec[:, gc + ch:gc + ch + 1], None, ALU.mult))
                    P.op("v", lambda E, sl=sl, ch=ch:
                         E.tensor_scalar(xrow[:, sl], xrow[:, sl],
                                         vec[:, bc + ch:bc + ch + 1], None, ALU.add), inc="V")
                for dsts, tmc in specs:
                    dst = dsts[ch]
                    tm = vec[:, tmc + ch:tmc + ch + 1]
                    P.op("v", lambda E, dst=dst: E.tensor_tensor(dst[:, 1:T], xrow[:, 1:T],
                                                                 xrow[:, 0:T - 1],
                                                                 ALU.subtract))
                    P.op("v", lambda E, dst=dst, tm=tm:
                         E.scalar_tensor_tensor(dst[:, 1:T], dst[:, 1:T], tm,
                                                xrow[:, 0:T - 1], ALU.mult, ALU.add))
                    P.op("v", lambda E, dst=dst, tm=tm:
                         E.tensor_scalar(dst[:, 0:1], xrow[:, 0:1], tm, None, ALU.mult),
                         inc="V")

    def _row(r):
        with P.scope():
            P.barrier()
            emit_lnstats_dram(P, C, xin, r, mean0, rstd0, f"l1s0{r}")
            R = [P.sbuf(f"l1a_R{c}_{r}", [128, T], BF16) for c in range(NCH)]

            with P.scope():
                mixr = [P.sbuf(f"l1a_mr{c}_{r}", [128, T], BF16) for c in range(NCH)]
                affine_mix_pass(xin, r, mean0, rstd0, VC["ln0g"], VC["ln0b"],
                                [(mixr, VC["tmr"])], f"l1p1r{r}")
                with P.scope():
                    wstr = P.sbuf(f"l1a_wstR{r}", [128, 1024], BF16)
                    P.barrier()
                    for c in range(NCH):
                        P.dbb("g", "LW")
                        wt = None
                        for k in range(NCH):
                            wt = P.op("g", lambda E, k=k, c=c:
                                      E.dma_start(wstr[:, k * 128:(k + 1) * 128],
                                                  W["l1a_Wr"][k * 128:(k + 1) * 128,
                                                              c * 128:(c + 1) * 128]),
                                      inc="LW", waits=[P.tok("T")] if k == 0 else [])
                        for tt_ in range(4):
                            sl = slice(tt_ * 512, (tt_ + 1) * 512)
                            mm = None
                            for k in range(NCH):
                                mm = P.op("t", lambda E, k=k, tt_=tt_:
                                          E.matmul(psE[:], wstr[:, k * 128:(k + 1) * 128],
                                                   mixr[k][:, tt_ * 512:(tt_ + 1) * 512],
                                                   start=(k == 0), stop=(k == NCH - 1)),
                                          inc="T" if k == NCH - 1 else None,
                                          waits=[wt, P.tok("V"), P.tok("S")] if k == 0 else [])
                            P.op("s", lambda E, c=c, sl=sl:
                                 E.activation(R[c][:, sl], psE[:], AF.Sigmoid), inc="S",
                                 waits=[mm])

            with P.scope():
                mixk = [P.sbuf(f"l1a_mk{c}_{r}", [128, T], BF16) for c in range(NCH)]
                mixv = [P.sbuf(f"l1a_mv{c}_{r}", [128, T], BF16) for c in range(NCH)]
                affine_mix_pass(xin, r, mean0, rstd0, VC["ln0g"], VC["ln0b"],
                                [(mixk, VC["tmk"]), (mixv, VC["tmv"])], f"l1p1kv{r}")
                ktile = P.sbuf(f"l1a_kt{r}", [128, T], F32)
                vtb = P.sbuf(f"l1a_vtb{r}", [128, T], BF16)
                karg = P.sbuf(f"l1a_karg{r}", [128, T], F32)
                ek = P.sbuf(f"l1a_ek{r}", [128, T], F32)
                eku = P.sbuf(f"l1a_eku{r}", [128, T], F32)
                tmpw = P.sbuf(f"l1a_tmpw{r}", [128, T], F32)
                T1t = P.sbuf(f"l1a_T1t{r}", [128, T], F32)
                ewt = P.sbuf(f"l1a_ewt{r}", [128, T], F32)
                ppm = P.sbuf(f"l1a_ppm{r}", [128, 1], F32)
                wstr2 = P.sbuf(f"l1a_wstKV{r}", [128, 2048], BF16)
                P.barrier()
                for c in range(NCH):
                    P.dbb("g", "LW")
                    wt = None
                    for pi, wn in enumerate(("l1a_Wk", "l1a_Wv")):
                        for k in range(NCH):
                            wt = P.op("g", lambda E, pi=pi, k=k, c=c, wn=wn:
                                      E.dma_start(wstr2[:, pi * 1024 + k * 128:
                                                        pi * 1024 + (k + 1) * 128],
                                                  W[wn][k * 128:(k + 1) * 128,
                                                        c * 128:(c + 1) * 128]),
                                      inc="LW",
                                      waits=[P.tok("T")] if (pi == 0 and k == 0) else [])
                    for tt_ in range(4):
                        sl = slice(tt_ * 512, (tt_ + 1) * 512)
                        mm = None
                        for k in range(NCH):
                            mm = P.op("t", lambda E, k=k, tt_=tt_:
                                      E.matmul(psE[:], wstr2[:, k * 128:(k + 1) * 128],
                                               mixk[k][:, tt_ * 512:(tt_ + 1) * 512],
                                               start=(k == 0), stop=(k == NCH - 1)),
                                      inc="T" if k == NCH - 1 else None,
                                      waits=[wt, P.tok("V")] if k == 0 else [])
                        P.op("v", lambda E, sl=sl: E.tensor_copy(ktile[:, sl], psE[:]),
                             inc="V", waits=[mm])
                        mm = None
                        for k in range(NCH):
                            mm = P.op("t", lambda E, k=k, tt_=tt_:
                                      E.matmul(psF[:], wstr2[:, 1024 + k * 128:
                                                             1024 + (k + 1) * 128],
                                               mixv[k][:, tt_ * 512:(tt_ + 1) * 512],
                                               start=(k == 0), stop=(k == NCH - 1)),
                                      inc="T" if k == NCH - 1 else None,
                                      waits=[wt, P.tok("V")] if k == 0 else [])
                        P.op("v", lambda E, sl=sl: E.tensor_copy(vtb[:, sl], psF[:]),
                             inc="V", waits=[mm])
                    P.op("v", lambda E: E.reduce_max(ppm[:], ktile[:], axis=AX.X), inc="V")
                    P.sp("v", 2)
                    P.op("v", lambda E: E.tensor_tensor(karg[:], ktile[:],
                                                        ppm[:, 0:1].broadcast_to([128, T]),
                                                        ALU.subtract), inc="V")
                    P.op("v", lambda E, c=c:
                         E.tensor_copy(ewt[:], vec[:, 80 + c:81 + c].broadcast_to([128, T])),
                         inc="V")
                    P.op("s", lambda E: E.activation(ek[:], karg[:], AF.Exp), inc="S",
                         waits=[P.tok("V")])
                    P.op("s", lambda E, c=c: E.activation(eku[:], karg[:], AF.Exp,
                                                          bias=vec[:, 72 + c:73 + c]),
                         inc="S")
                    P.op("v", lambda E: E.tensor_tensor(tmpw[:], eku[:], vtb[:], ALU.mult),
                         waits=[P.tok("S")])
                    P.op("v", lambda E: E.tensor_tensor_scan(karg[:], ewt[:], ek[:], 0.0,
                                                             ALU.mult, ALU.add))
                    P.op("v", lambda E: E.tensor_tensor(ek[:], ek[:], vtb[:], ALU.mult))
                    P.op("v", lambda E: E.tensor_tensor_scan(T1t[:], ewt[:], ek[:], 0.0,
                                                             ALU.mult, ALU.add))
                    P.op("v", lambda E: E.tensor_tensor(tmpw[:, 1:T], tmpw[:, 1:T],
                                                        T1t[:, 0:T - 1], ALU.add))
                    P.op("v", lambda E: E.tensor_tensor(eku[:, 1:T], eku[:, 1:T],
                                                        karg[:, 0:T - 1], ALU.add), inc="V")
                    P.op("v", lambda E: E.reciprocal(eku[:], eku[:]), inc="V")
                    P.op("v", lambda E: E.tensor_tensor(tmpw[:], tmpw[:], eku[:], ALU.mult))
                    P.op("v", lambda E, c=c: E.tensor_tensor(R[c][:, :], tmpw[:], R[c][:, :],
                                                             ALU.mult), inc="V")

            with P.scope():
                wo = P.sbuf(f"l1a_wo{r}", [128, 8 * 1024], BF16)
                xf = P.sbuf(f"l1a_xf{r}", [128, 512], F32)
                ot = P.sbuf(f"l1a_ot{r}", [128, 512], F32)
                P.barrier()
                wt = None
                for k in range(NCH):
                    wt = P.op("g", lambda E, k=k: E.dma_start(wo[:, k * 1024:(k + 1) * 1024],
                                                              W["l1a_Wo"][k * 128:(k + 1) * 128, :]),
                              inc="LG")
                for ch in range(NCH):
                    for tt_ in range(4):
                        sl = slice(tt_ * 512, (tt_ + 1) * 512)
                        ld = P.op("y", lambda E, r=r, ch=ch, tt_=tt_:
                                  E.dma_start(xf[:], xin[r, ch * 128:(ch + 1) * 128,
                                                         tt_ * 512:(tt_ + 1) * 512]),
                                  inc="LY", waits=[P.tok("V")])
                        mm = None
                        for k in range(NCH):
                            mm = P.op("t", lambda E, ch=ch, k=k, tt_=tt_:
                                      E.matmul(psE[:], wo[:, k * 1024 + ch * 128:
                                                          k * 1024 + (ch + 1) * 128],
                                               R[k][:, tt_ * 512:(tt_ + 1) * 512],
                                               start=(k == 0), stop=(k == NCH - 1)),
                                      inc="T" if k == NCH - 1 else None,
                                      waits=[wt, P.tok("V")] if k == 0 else [])
                        a1 = P.op("v", lambda E: E.tensor_tensor(ot[:], psE[:], xf[:],
                                                                 ALU.add),
                                  inc="V", waits=[mm, ld, P.tok("ST")])
                        P.op("y", lambda E, r=r, ch=ch, tt_=tt_:
                             E.dma_start(xiscr[r, ch * 128:(ch + 1) * 128,
                                               tt_ * 512:(tt_ + 1) * 512], ot[:]),
                             inc="ST", waits=[a1])

        with P.scope():
            P.barrier()
            emit_lnstats_dram(P, C, xiscr, r, mean1, rstd1, f"l1s1{r}")
            mixcr = [P.sbuf(f"l1a_mcr{c}_{r}", [128, T], BF16) for c in range(NCH)]

            with P.scope():
                mixck = [P.sbuf(f"l1a_mck{c}_{r}", [128, T], BF16) for c in range(NCH)]
                affine_mix_pass(xiscr, r, mean1, rstd1, VC["ln1g"], VC["ln1b"],
                                [(mixck, VC["cmk"]), (mixcr, VC["cmr"])], f"l1p4{r}")
                with P.scope():
                    wckc = P.sbuf(f"l1a_wckc{r}", [128, 2048], BF16)
                    hst1 = P.sbuf(f"l1a_hst1{r}", [128, 512], F32)
                    hst2 = P.sbuf(f"l1a_hst2{r}", [128, 512], BF16)
                    P.barrier()
                    for hk in range(32):
                        buf = hk % 2
                        P.dbb("g", "LW")
                        wt = None
                        for k in range(NCH):
                            wt = P.op("g", lambda E, hk=hk, k=k, buf=buf:
                                      E.dma_start(wckc[:, buf * 1024 + k * 128:
                                                        buf * 1024 + (k + 1) * 128],
                                                  W["l1a_Wck"][k * 128:(k + 1) * 128,
                                                               hk * 128:(hk + 1) * 128]),
                                      inc="LW", waits=[P.tok("T")] if k == 0 else [])
                        for tt_ in range(4):
                            mm = None
                            for k in range(NCH):
                                mm = P.op("t", lambda E, tt_=tt_, k=k, buf=buf:
                                          E.matmul(psE[:], wckc[:, buf * 1024 + k * 128:
                                                                 buf * 1024 + (k + 1) * 128],
                                                   mixck[k][:, tt_ * 512:(tt_ + 1) * 512],
                                                   start=(k == 0), stop=(k == NCH - 1)),
                                          inc="T" if k == NCH - 1 else None,
                                          waits=[wt, P.tok("S"), P.tok("V")] if k == 0 else [])
                            a1 = P.op("v", lambda E: E.tensor_scalar(hst1[:], psE[:], 0.0,
                                                                     None, ALU.max),
                                      inc="V", waits=[mm, P.tok("S")])
                            a2 = P.op("s", lambda E: E.activation(hst2[:], hst1[:],
                                                                  AF.Square), inc="S",
                                      waits=[a1, P.tok("SH")])
                            P.dbb("g", "SH")
                            P.op("g", lambda E, hk=hk, tt_=tt_, r=r:
                                 E.dma_start(hscr[hk * 128:(hk + 1) * 128,
                                                  r * T + tt_ * 512:r * T + (tt_ + 1) * 512],
                                             hst2[:]),
                                 inc="SH", waits=[a2])

            rr = [P.sbuf(f"l1a_rr{c}_{r}", [128, T], BF16) for c in range(NCH)]
            with P.scope():
                wcr = P.sbuf(f"l1a_wcr{r}", [128, 8 * 1024], BF16)
                P.barrier()
                wt = None
                for k in range(NCH):
                    wt = P.op("g", lambda E, k=k: E.dma_start(wcr[:, k * 1024:(k + 1) * 1024],
                                                              W["l1a_Wcr"][k * 128:(k + 1) * 128, :]),
                              inc="LG")
                for c in range(NCH):
                    for tt_ in range(4):
                        sl = slice(tt_ * 512, (tt_ + 1) * 512)
                        mm = None
                        for k in range(NCH):
                            mm = P.op("t", lambda E, c=c, k=k, tt_=tt_:
                                      E.matmul(psE[:], wcr[:, k * 1024 + c * 128:
                                                           k * 1024 + (c + 1) * 128],
                                               mixcr[k][:, tt_ * 512:(tt_ + 1) * 512],
                                               start=(k == 0), stop=(k == NCH - 1)),
                                      inc="T" if k == NCH - 1 else None,
                                      waits=[wt, P.tok("S")] if k == 0 else [])
                        P.op("s", lambda E, c=c, sl=sl:
                             E.activation(rr[c][:, sl], psE[:], AF.Sigmoid), inc="S",
                             waits=[mm])

            with P.scope():
                wcv = P.sbuf(f"l1a_wcv{r}", [128, 32 * 1024], BF16)
                h2k = [P.sbuf(f"l1a_h2k{k}_{r}", [128, 512], BF16) for k in range(32)]
                xf2 = P.sbuf(f"l1a_xf2{r}", [128, 512], F32)
                xif = P.sbuf(f"l1a_xif{r}", [128, 512], F32)
                zt = P.sbuf(f"l1a_zt{r}", [128, 512], F32)
                P.barrier()
                wt = None
                for k in range(32):
                    wt = P.op("g", lambda E, k=k: E.dma_start(wcv[:, k * 1024:(k + 1) * 1024],
                                                              W["l1a_Wcv"][k * 128:(k + 1) * 128, :]),
                              inc="LG")
                for tt_ in range(4):
                    P.dbb("y", "LH")
                    lh = None
                    for k in range(32):
                        lh = P.op("y", lambda E, k=k, tt_=tt_, r=r:
                                  E.dma_start(h2k[k][:], hscr[k * 128:(k + 1) * 128,
                                                              r * T + tt_ * 512:
                                                              r * T + (tt_ + 1) * 512]),
                                  inc="LH", waits=[P.tok("T")] if k == 0 else [])
                    for ch in range(NCH):
                        sl = slice(tt_ * 512, (tt_ + 1) * 512)
                        mm = None
                        for k in range(32):
                            mm = P.op("t", lambda E, ch=ch, k=k:
                                      E.matmul(psE[:], wcv[:, k * 1024 + ch * 128:
                                                           k * 1024 + (ch + 1) * 128],
                                               h2k[k][:], start=(k == 0), stop=(k == 31)),
                                      inc="T" if k == 31 else None,
                                      waits=[wt, lh, P.tok("V")] if k == 0 else [])
                        ld1 = P.op("y", lambda E, r=r, ch=ch, tt_=tt_:
                                   E.dma_start(xf2[:], xin[r, ch * 128:(ch + 1) * 128,
                                                           tt_ * 512:(tt_ + 1) * 512]),
                                   inc="LY", waits=[P.tok("V")])
                        ld2 = P.op("y", lambda E, r=r, ch=ch, tt_=tt_:
                                   E.dma_start(xif[:], xiscr[r, ch * 128:(ch + 1) * 128,
                                                             tt_ * 512:(tt_ + 1) * 512]),
                                   inc="LY")
                        P.op("v", lambda E, ch=ch, sl=sl:
                             E.tensor_tensor(zt[:], psE[:], rr[ch][:, sl], ALU.mult),
                             waits=[mm, P.tok("S"), ld2, P.tok("ST")])
                        P.op("v", lambda E: E.tensor_tensor(zt[:], zt[:], xif[:], ALU.add))
                        a3 = P.op("v", lambda E: E.tensor_tensor(zt[:], zt[:], xf2[:],
                                                                 ALU.add), inc="V",
                                  waits=[ld1])
                        P.op("y", lambda E, r=r, ch=ch, tt_=tt_:
                             E.dma_start(zout[r, ch * 128:(ch + 1) * 128,
                                              tt_ * 512:(tt_ + 1) * 512], zt[:]),
                             inc="ST", waits=[a3])

    for r in range(NR):
        _row(r)


def build_l1a_program():
    P = Prog()
    P.alloc_psum_pool()
    P.alloc_spacers()
    xin = P.dram("x2T", [NR, D, T], F32, kind="ExternalInput")
    xout = P.dram("x3T", [NR, D, T], F32, kind="ExternalOutput")
    zscr = P.dram("zscr", [NR, D, T], F32, kind="Internal")
    xiscr = P.dram("xiscr", [NR, D, T], F32, kind="Internal")
    hscr = P.dram("hscr", [4096, TT], BF16, kind="Internal")
    C = declare_consts(P, l0=False)
    W = declare_l1a_weights(P)
    with P.scope():
        emit_l1_attn(P, W, C, xin, zscr, hscr, xiscr)
    with P.scope():
        emit_post_ln(P, C, zscr, xout, "ln2")
    P.barrier()
    P.wait("y", P.tok("ST"))
    return P.build()



def prep_l1_ffn_weights(d):
    w = {}
    W1 = np.asarray(d["L1_ffn_W1"], np.float32)
    W2 = np.asarray(d["L1_ffn_W2"], np.float32)
    b1 = np.asarray(d["L1_ffn_b1"], np.float32)
    b2 = np.asarray(d["L1_ffn_b2"], np.float32)
    w["l1f_W1"] = np.ascontiguousarray(W1.reshape(8 * 1024, 4096))
    w["l1f_W2"] = np.ascontiguousarray(W2.reshape(8 * 4096, 1024))
    w["l1f_b1r"] = np.ascontiguousarray(b1.reshape(8, 32, 128).transpose(0, 2, 1)
                                        .reshape(8 * 128, 32))
    w["l1f_b2r"] = np.ascontiguousarray(b2.reshape(8, 8, 128).transpose(0, 2, 1)
                                        .reshape(8 * 128, 8))
    w["l1f_Wr"] = np.ascontiguousarray(np.asarray(d["L1_ffn_Wr"], np.float32))
    w["l1f_br"] = np.ascontiguousarray(np.asarray(d["L1_ffn_br"], np.float32)
                                       .reshape(1, 8))
    w["l1f_g"] = np.ascontiguousarray(np.asarray(d["L1_ffn_gamma"], np.float32)
                                      .reshape(D, 1))
    w["l1f_be"] = np.ascontiguousarray(np.asarray(d["L1_ffn_beta"], np.float32)
                                       .reshape(D, 1))
    j = np.arange(8)
    w["c_ltri8"] = np.ascontiguousarray((j[None, :] < j[:, None]).astype(np.float32))
    w["c_iota8"] = np.ascontiguousarray(j.astype(np.float32).reshape(1, 8))
    w["c_iota128"] = np.ascontiguousarray(np.arange(128, dtype=np.float32).reshape(128, 1))
    return w


def declare_l1f_weights(P):
    W = {}
    W["l1f_W1"] = P.dram("l1f_W1", [8 * 1024, 4096], F32, kind="ExternalInput")
    W["l1f_W2"] = P.dram("l1f_W2", [8 * 4096, 1024], F32, kind="ExternalInput")
    W["l1f_b1r"] = P.dram("l1f_b1r", [8 * 128, 32], F32, kind="ExternalInput")
    W["l1f_b2r"] = P.dram("l1f_b2r", [8 * 128, 8], F32, kind="ExternalInput")
    W["l1f_Wr"] = P.dram("l1f_Wr", [D, 8], F32, kind="ExternalInput")
    W["l1f_br"] = P.dram("l1f_br", [1, 8], F32, kind="ExternalInput")
    W["l1f_g"] = P.dram("l1f_g", [D, 1], F32, kind="ExternalInput")
    W["l1f_be"] = P.dram("l1f_be", [D, 1], F32, kind="ExternalInput")
    W["c_ltri8"] = P.dram("c_ltri8", [8, 8], F32, kind="ExternalInput")
    W["c_iota8"] = P.dram("c_iota8", [1, 8], F32, kind="ExternalInput")
    W["c_iota128"] = P.dram("c_iota128", [128, 1], F32, kind="ExternalInput")
    return W


def emit_l1_ffn(P, W, C, xin, zout, hscr, yscr):
    ones1x = C["ones1x128"]
    onesb = C["ones128_bf"]
    psL, psF, psBC, psB2, psE = P.psA, P.psF, P.psC, P.psD, P.psE

    vec = P.sbuf("l1f_vec", [128, 16], F32)
    _loadvec(P, vec, 0, W["l1f_g"], 8)
    _loadvec(P, vec, 8, W["l1f_be"], 8)
    wr_s = P.sbuf("l1f_wr", [128, 64], F32)
    br_s = P.sbuf("l1f_brs", [1, 8], F32)
    ltri = P.sbuf("l1f_ltri", [8, 8], F32)
    iota8 = P.sbuf("l1f_iota8", [1, 8], F32)
    iota128 = P.sbuf("l1f_iota128", [128, 1], F32)
    wtk = []
    for c in range(NCH):
        wtk.append(P.op("y", lambda E, c=c: E.dma_start(wr_s[:, c * 8:(c + 1) * 8],
                                                        W["l1f_Wr"][c * 128:(c + 1) * 128, :]),
                        inc="LY"))
    for dst, nm in ((br_s, "l1f_br"), (ltri, "c_ltri8"), (iota8, "c_iota8"),
                    (iota128, "c_iota128")):
        wtk.append(P.op("y", lambda E, dst=dst, nm=nm: E.dma_start(dst[:], W[nm][:]),
                        inc="LY"))
    wtok = wtk[-1]

    lgt = P.sbuf("l1f_lgt", [1, 16], F32)
    rw = P.sbuf("l1f_rw", [1, 16], F32)
    cws = P.sbuf("l1f_cws", [1, 4], F32)
    efs = P.sbuf("l1f_efs", [1, 4], F32)
    cwb = P.sbuf("l1f_cwb", [128, 4], F32)
    b1c = P.sbuf("l1f_b1c", [128, 128], F32)
    b2c = P.sbuf("l1f_b2c", [128, 32], F32)
    idxW1 = [[[P.sbuf(f"l1f_i1_{r}_{s}_{k}", [128, 1], I32) for k in range(8)]
              for s in range(2)] for r in range(NR)]
    idxW2 = [[[P.sbuf(f"l1f_i2_{r}_{s}_{k}", [128, 1], I32) for k in range(32)]
              for s in range(2)] for r in range(NR)]
    idxB = [[P.sbuf(f"l1f_ib_{r}_{s}", [128, 1], I32) for s in range(2)]
            for r in range(NR)]

    with P.scope():
        srow = P.sbuf("l1f_srow", [128, T], F32)
        msum = P.sbuf("l1f_msum", [128, 16], F32)
        pacc = P.sbuf("l1f_pacc", [128, 8], F32)
        ptmp = P.sbuf("l1f_ptmp", [128, 8], F32)
        phi = P.sbuf("l1f_phi", [128, 8], BF16)
        plo = P.sbuf("l1f_plo", [128, 8], BF16)
        plo2 = P.sbuf("l1f_plo2", [128, 8], BF16)
        gt8 = P.sbuf("l1f_gt8", [8, 8], F32)
        eq8 = P.sbuf("l1f_eq8", [8, 8], F32)
        rank = P.sbuf("l1f_rank", [8, 1], F32)
        sel = P.sbuf("l1f_sel", [8, 1], F32)
        rwc = P.sbuf("l1f_rwc", [8, 1], F32)
        erw = P.sbuf("l1f_erw", [8, 1], F32)
        eselr = P.sbuf("l1f_eselr", [1, 8], F32)
        rankr = P.sbuf("l1f_rankr", [1, 8], F32)
        sc1 = P.sbuf("l1f_sc1", [1, 1], F32)
        sc2 = P.sbuf("l1f_sc2", [1, 1], F32)
        efb = P.sbuf("l1f_efb", [128, 1], F32)
        idf = P.sbuf("l1f_idf", [128, 1], F32)

        P.barrier()
        P.auto_sp = True
        for r in range(NR):
            for c in range(NCH):
                ld = P.op("g", lambda E, r=r, c=c:
                          E.dma_start(srow[:], xin[r, c * 128:(c + 1) * 128, :]),
                          inc="LX", waits=[P.tok("V")])
                P.op("v", lambda E, r=r, c=c: E.reduce_sum(msum[:, r * 8 + c:r * 8 + c + 1],
                                                           srow[:], axis=AX.X),
                     inc="V", waits=[ld])
        P.op("v", lambda E: E.tensor_scalar(msum[:], msum[:], 1.0 / T, None, ALU.mult),
             inc="V", waits=[wtok])
        P.sp("v", 2)
        P.dbg("msum", msum[:, :], [128, 16])
        for r in range(NR):
            for c in range(NCH):
                if c == 0:
                    P.op("v", lambda E, r=r: E.tensor_tensor(
                        pacc[:], wr_s[:, 0:8],
                        msum[:, r * 8:r * 8 + 1].broadcast_to([128, 8]), ALU.mult))
                else:
                    P.op("v", lambda E, r=r, c=c: E.tensor_tensor(
                        ptmp[:], wr_s[:, c * 8:(c + 1) * 8],
                        msum[:, r * 8 + c:r * 8 + c + 1].broadcast_to([128, 8]), ALU.mult))
                    P.op("v", lambda E: E.tensor_tensor(pacc[:], pacc[:], ptmp[:], ALU.add))
            P.op("v", lambda E: E.tensor_copy(phi[:], pacc[:]))
            P.op("v", lambda E: E.tensor_tensor(pacc[:], pacc[:], phi[:], ALU.subtract))
            P.op("v", lambda E: E.tensor_copy(plo[:], pacc[:]))
            P.op("v", lambda E: E.tensor_tensor(pacc[:], pacc[:], plo[:], ALU.subtract))
            P.op("v", lambda E: E.tensor_copy(plo2[:], pacc[:]), inc="V")
            pv = P.spt("v")
            mm = None
            for pt, tile_ in enumerate((phi, plo, plo2)):
                mm = P.op("t", lambda E, pt=pt, tile_=tile_:
                          E.matmul(psL[0:1, 0:8], onesb[:, 0:1], tile_[:],
                                   start=(pt == 0), stop=(pt == 2)),
                          inc="T" if pt == 2 else None, waits=[pv] if pt == 0 else [])
            P.wait("v", mm)
            P.sp("v", 2)
            P.op("v", lambda E, r=r: E.tensor_tensor(lgt[:, r * 8:(r + 1) * 8],
                                                     psL[0:1, 0:8], br_s[:], ALU.add),
                 inc="V")
            P.sp("v", 2)
            P.op("v", lambda E, r=r: E.reduce_max(sc1[:], lgt[:, r * 8:(r + 1) * 8],
                                                  axis=AX.X), inc="V")
            P.sp("v", 2)
            P.op("v", lambda E, r=r: E.tensor_tensor(rw[:, r * 8:(r + 1) * 8],
                                                     lgt[:, r * 8:(r + 1) * 8],
                                                     sc1[:, 0:1].broadcast_to([1, 8]),
                                                     ALU.subtract), inc="V")
            P.sp("v", 2)
            P.wait("s", P.tok("V"))
            P.sp("s", 2)
            P.op("s", lambda E, r=r: E.activation(rw[:, r * 8:(r + 1) * 8],
                                                  rw[:, r * 8:(r + 1) * 8], AF.Exp),
                 inc="S")
            P.wait("v", P.tok("S"))
            P.sp("v", 2)
            P.op("v", lambda E, r=r: E.reduce_sum(sc2[:], rw[:, r * 8:(r + 1) * 8],
                                                  axis=AX.X), inc="V")
            P.sp("v", 2)
            P.op("v", lambda E: E.reciprocal(sc2[:], sc2[:]), inc="V")
            P.sp("v", 2)
            P.op("v", lambda E, r=r: E.tensor_tensor(rw[:, r * 8:(r + 1) * 8],
                                                     rw[:, r * 8:(r + 1) * 8],
                                                     sc2[:, 0:1].broadcast_to([1, 8]),
                                                     ALU.mult), inc="V")
            sptk = P.spt("v")
            bc = P.op("t", lambda E, r=r: E.matmul(psF[0:8, 0:8], ones1x[0:1, 0:8],
                                                   rw[0:1, r * 8:(r + 1) * 8],
                                                   start=True, stop=True), inc="T",
                      waits=[sptk])
            P.dbb("g", "LQ")
            dq = P.op("g", lambda E, r=r: E.dma_start(rwc[:], rw[0:1, r * 8:(r + 1) * 8]),
                      inc="LQ", waits=[sptk])
            P.wait("v", bc)
            P.wait("v", dq)
            P.sp("v", 2)
            P.op("v", lambda E: E.tensor_tensor(gt8[:], psF[0:8, 0:8],
                                                rwc[:, 0:1].broadcast_to([8, 8]),
                                                ALU.is_gt))
            P.op("v", lambda E: E.tensor_tensor(eq8[:], psF[0:8, 0:8],
                                                rwc[:, 0:1].broadcast_to([8, 8]),
                                                ALU.is_equal))
            P.op("v", lambda E: E.tensor_tensor(eq8[:], eq8[:], ltri[:], ALU.mult))
            P.op("v", lambda E: E.reduce_sum(rank[:], gt8[:], axis=AX.X))
            P.op("v", lambda E: E.reduce_sum(sel[:], eq8[:], axis=AX.X), inc="V")
            P.sp("v", 2)
            P.op("v", lambda E: E.tensor_tensor(rank[:], rank[:], sel[:], ALU.add), inc="V")
            P.sp("v", 2)
            P.op("v", lambda E: E.tensor_scalar(sel[:], rank[:], 1.0, None, ALU.is_le),
                 inc="V")
            P.wait("s", P.tok("V"))
            P.sp("s", 2)
            P.op("s", lambda E: E.activation(erw[:], rwc[:], AF.Exp), inc="S")
            P.wait("v", P.tok("S"))
            P.sp("v", 2)
            P.op("v", lambda E: E.tensor_tensor(erw[:], erw[:], sel[:], ALU.mult), inc="V")
            sptk2 = P.spt("v")
            P.dbb("g", "LQ")
            d1 = P.op("g", lambda E: E.dma_start(eselr[:], erw[:]), inc="LQ",
                      waits=[sptk2])
            d2 = P.op("g", lambda E: E.dma_start(rankr[:], rank[:]), inc="LQ")
            P.wait("v", d2)
            P.sp("v", 2)
            P.op("v", lambda E: E.reduce_sum(sc1[:], eselr[:], axis=AX.X), inc="V")
            P.sp("v", 2)
            P.op("v", lambda E: E.reciprocal(sc1[:], sc1[:]), inc="V")
            P.sp("v", 2)
            P.op("v", lambda E: E.tensor_tensor(eselr[:], eselr[:],
                                                sc1[:, 0:1].broadcast_to([1, 8]),
                                                ALU.mult), inc="V")
            P.sp("v", 2)
            for s in range(2):
                col = r * 2 + s
                P.op("v", lambda E, s=s: E.tensor_scalar(gt8[0:1, :], rankr[:], float(s),
                                                         None, ALU.is_equal), inc="V")
                P.sp("v", 2)
                P.op("v", lambda E: E.tensor_tensor(eq8[0:1, :], eselr[:], gt8[0:1, :],
                                                    ALU.mult))
                P.op("v", lambda E, col=col: E.reduce_sum(cws[:, col:col + 1],
                                                          eq8[0:1, :], axis=AX.X))
                P.op("v", lambda E: E.tensor_tensor(eq8[0:1, :], iota8[:], gt8[0:1, :],
                                                    ALU.mult))
                P.op("v", lambda E, col=col: E.reduce_sum(efs[:, col:col + 1],
                                                          eq8[0:1, :], axis=AX.X), inc="V")
                sptk3 = P.spt("v")
                b1_ = P.op("t", lambda E, col=col: E.matmul(psF[:, 0:1], ones1x[0:1, :],
                                                            cws[0:1, col:col + 1],
                                                            start=True, stop=True),
                           inc="T", waits=[sptk3])
                P.wait("v", b1_)
                P.sp("v", 2)
                P.op("v", lambda E, col=col: E.tensor_copy(cwb[:, col:col + 1],
                                                           psF[:, 0:1]), inc="V")
                b2_ = P.op("t", lambda E, col=col: E.matmul(psF[:, 0:1], ones1x[0:1, :],
                                                            efs[0:1, col:col + 1],
                                                            start=True, stop=True),
                           inc="T", waits=[P.tok("V")])
                P.wait("v", b2_)
                P.sp("v", 2)
                P.op("v", lambda E: E.tensor_copy(efb[:], psF[:, 0:1]), inc="V")
                P.op("v", lambda E: E.tensor_scalar(efb[:], efb[:], 0.0, 7.0, ALU.max,
                                                    ALU.min), inc="V")
                P.sp("v", 2)
                for k in range(8):
                    P.op("v", lambda E, k=k: E.tensor_scalar(idf[:], efb[:], 1024.0,
                                                             float(k * 128), ALU.mult,
                                                             ALU.add))
                    P.op("v", lambda E: E.tensor_tensor(idf[:], idf[:], iota128[:],
                                                        ALU.add))
                    P.op("v", lambda E, r=r, s=s, k=k: E.tensor_copy(idxW1[r][s][k][:],
                                                                     idf[:]), inc="V")
                for k in range(32):
                    P.op("v", lambda E, k=k: E.tensor_scalar(idf[:], efb[:], 4096.0,
                                                             float(k * 128), ALU.mult,
                                                             ALU.add))
                    P.op("v", lambda E: E.tensor_tensor(idf[:], idf[:], iota128[:],
                                                        ALU.add))
                    P.op("v", lambda E, r=r, s=s, k=k: E.tensor_copy(idxW2[r][s][k][:],
                                                                     idf[:]), inc="V")
                P.op("v", lambda E: E.tensor_scalar(idf[:], efb[:], 128.0, None, ALU.mult))
                P.op("v", lambda E: E.tensor_tensor(idf[:], idf[:], iota128[:], ALU.add))
                P.op("v", lambda E, r=r, s=s: E.tensor_copy(idxB[r][s][:], idf[:]),
                     inc="V")
                P.sp("v", 2)
                P.dbb("g", "LQ")
                g1 = P.op("g", lambda E, r=r, s=s, col=col: E.indirect_dma_start(
                    b1c[:, col * 32:(col + 1) * 32], None, W["l1f_b1r"][:, :],
                    bass.IndirectOffsetOnAxis(idxB[r][s][:, 0:1], 0)), inc="LQ",
                    waits=[P.tok("V")])
                P.op("g", lambda E, r=r, s=s, col=col: E.indirect_dma_start(
                    b2c[:, col * 8:(col + 1) * 8], None, W["l1f_b2r"][:, :],
                    bass.IndirectOffsetOnAxis(idxB[r][s][:, 0:1], 0)), inc="LQ")
        P.auto_sp = False
        P.dbg("lgt", lgt[:, :], [1, 16])
        P.dbg("rw", rw[:, :], [1, 16])
        P.dbg("efs", efs[:, :], [1, 4])
        P.dbg("cws", cws[:, :], [1, 4])

    mean = P.sbuf("l1f_mean", [1, T], F32)
    rstd = P.sbuf("l1f_rstd", [1, T], F32)

    def _row(r):
        with P.scope():
            acc = [P.sbuf(f"l1f_acc{c}_{r}", [128, T], F32) for c in range(NCH)]
            for s in range(2):
                col = r * 2 + s
                with P.scope():
                    x3bf = [P.sbuf(f"l1f_x3b{c}_{r}{s}", [128, T], BF16)
                            for c in range(NCH)]
                    w1g = [P.sbuf(f"l1f_w1g{k}_{r}{s}", [128, 4096], BF16)
                           for k in range(8)]
                    hst2 = P.sbuf(f"l1f_hst2_{r}{s}", [128, 512], BF16)
                    P.barrier()
                    P.dbb("g", "LW")
                    wt = None
                    for k in range(8):
                        wt = P.op("g", lambda E, r=r, s=s, k=k: E.indirect_dma_start(
                            w1g[k][:], None, W["l1f_W1"][:, :],
                            bass.IndirectOffsetOnAxis(idxW1[r][s][k][:, 0:1], 0)),
                            inc="LW")
                    xt = None
                    for c in range(NCH):
                        xt = P.op("g", lambda E, r=r, c=c:
                                  E.dma_start(x3bf[c][:], xin[r, c * 128:(c + 1) * 128, :]),
                                  inc="LX")
                    for hk in range(32):
                        for tt_ in range(4):
                            mm = None
                            for k in range(8):
                                mm = P.op("t", lambda E, k=k, hk=hk, tt_=tt_:
                                          E.matmul(psE[:], w1g[k][:, hk * 128:(hk + 1) * 128],
                                                   x3bf[k][:, tt_ * 512:(tt_ + 1) * 512],
                                                   start=(k == 0), stop=(k == 7)),
                                          inc="T" if k == 7 else None,
                                          waits=[wt, xt, P.tok("S")] if k == 0 else [])
                            a2 = P.op("s", lambda E, col=col, hk=hk:
                                      E.activation(hst2[:], psE[:], AF.Gelu,
                                                   bias=b1c[:, col * 32 + hk:col * 32 + hk + 1]),
                                      inc="S", waits=[mm, P.tok("SH")])
                            P.dbb("g", "SH")
                            P.op("g", lambda E, hk=hk, tt_=tt_:
                                 E.dma_start(hscr[hk * 128:(hk + 1) * 128,
                                                  tt_ * 512:(tt_ + 1) * 512], hst2[:]),
                                 inc="SH", waits=[a2])
                with P.scope():
                    w2g = [P.sbuf(f"l1f_w2g{k}_{r}{s}", [128, 1024], BF16)
                           for k in range(32)]
                    h2k = [P.sbuf(f"l1f_h2k{k}_{r}{s}", [128, 512], BF16)
                           for k in range(32)]
                    otmp = P.sbuf(f"l1f_otmp_{r}{s}", [128, 512], F32)
                    P.barrier()
                    P.dbb("g", "LW")
                    wt = None
                    for k in range(32):
                        wt = P.op("g", lambda E, r=r, s=s, k=k: E.indirect_dma_start(
                            w2g[k][:], None, W["l1f_W2"][:, :],
                            bass.IndirectOffsetOnAxis(idxW2[r][s][k][:, 0:1], 0)),
                            inc="LW")
                    for tt_ in range(4):
                        P.dbb("y", "LH")
                        lh = None
                        for k in range(32):
                            lh = P.op("y", lambda E, k=k, tt_=tt_:
                                      E.dma_start(h2k[k][:], hscr[k * 128:(k + 1) * 128,
                                                                  tt_ * 512:(tt_ + 1) * 512]),
                                      inc="LH", waits=[P.tok("T")] if k == 0 else [])
                        for ch in range(NCH):
                            sl = slice(tt_ * 512, (tt_ + 1) * 512)
                            mm = None
                            for k in range(32):
                                mm = P.op("t", lambda E, ch=ch, k=k:
                                          E.matmul(psE[:], w2g[k][:, ch * 128:(ch + 1) * 128],
                                                   h2k[k][:], start=(k == 0), stop=(k == 31)),
                                          inc="T" if k == 31 else None,
                                          waits=[wt, lh, P.tok("V")] if k == 0 else [])
                            if s == 0:
                                P.op("v", lambda E, ch=ch, sl=sl, col=col:
                                     E.scalar_tensor_tensor(
                                         acc[ch][:, sl], psE[:],
                                         b2c[:, col * 8 + ch:col * 8 + ch + 1],
                                         cwb[:, col:col + 1].broadcast_to([128, 512]),
                                         ALU.add, ALU.mult), inc="V", waits=[mm])
                            else:
                                P.op("v", lambda E, ch=ch, col=col:
                                     E.scalar_tensor_tensor(
                                         otmp[:], psE[:],
                                         b2c[:, col * 8 + ch:col * 8 + ch + 1],
                                         cwb[:, col:col + 1].broadcast_to([128, 512]),
                                         ALU.add, ALU.mult), waits=[mm])
                                P.op("v", lambda E, ch=ch, sl=sl:
                                     E.tensor_tensor(acc[ch][:, sl], acc[ch][:, sl],
                                                     otmp[:], ALU.add), inc="V")
            with P.scope():
                xf = P.sbuf(f"l1f_xf_{r}", [128, 512], F32)
                yt = P.sbuf(f"l1f_yt_{r}", [128, 512], F32)
                P.barrier()
                for ch in range(NCH):
                    for tt_ in range(4):
                        sl = slice(tt_ * 512, (tt_ + 1) * 512)
                        ld = P.op("y", lambda E, r=r, ch=ch, tt_=tt_:
                                  E.dma_start(xf[:], xin[r, ch * 128:(ch + 1) * 128,
                                                         tt_ * 512:(tt_ + 1) * 512]),
                                  inc="LY", waits=[P.tok("V")])
                        a1 = P.op("v", lambda E, ch=ch, sl=sl:
                                  E.tensor_tensor(yt[:], acc[ch][:, sl], xf[:], ALU.add),
                                  inc="V", waits=[ld, P.tok("ST")])
                        P.op("y", lambda E, r=r, ch=ch, tt_=tt_:
                             E.dma_start(yscr[r, ch * 128:(ch + 1) * 128,
                                              tt_ * 512:(tt_ + 1) * 512], yt[:]),
                             inc="ST", waits=[a1])
        with P.scope():
            P.barrier()
            emit_lnstats_dram(P, C, yscr, r, mean, rstd, f"l1fs{r}")
            yf = P.sbuf(f"l1f_yf_{r}", [128, 512], F32)
            xf3 = P.sbuf(f"l1f_xf3_{r}", [128, 512], F32)
            ot = P.sbuf(f"l1f_ot_{r}", [128, 512], F32)
            for tt_ in range(4):
                P.op("t", lambda E, tt_=tt_: E.matmul(psBC[:], ones1x[0:1, :],
                                                      mean[0:1, tt_ * 512:(tt_ + 1) * 512],
                                                      start=True, stop=True),
                     waits=[P.tok("V")])
                b2_ = P.op("t", lambda E, tt_=tt_: E.matmul(psB2[:], ones1x[0:1, :],
                                                            rstd[0:1, tt_ * 512:(tt_ + 1) * 512],
                                                            start=True, stop=True), inc="T")
                for ch in range(NCH):
                    ld1 = P.op("y", lambda E, r=r, ch=ch, tt_=tt_:
                               E.dma_start(yf[:], yscr[r, ch * 128:(ch + 1) * 128,
                                                       tt_ * 512:(tt_ + 1) * 512]),
                               inc="LY", waits=[P.tok("V")])
                    ld2 = P.op("y", lambda E, r=r, ch=ch, tt_=tt_:
                               E.dma_start(xf3[:], xin[r, ch * 128:(ch + 1) * 128,
                                                       tt_ * 512:(tt_ + 1) * 512]),
                               inc="LY")
                    P.op("v", lambda E: E.tensor_tensor(ot[:], yf[:], psBC[:],
                                                        ALU.subtract),
                         waits=[ld1, ld2, b2_, P.tok("ST")])
                    P.op("v", lambda E: E.tensor_tensor(ot[:], ot[:], psB2[:], ALU.mult))
                    P.op("v", lambda E, ch=ch: E.scalar_tensor_tensor(
                        ot[:], ot[:], vec[:, ch:ch + 1], xf3[:], ALU.mult, ALU.add))
                    a4 = P.op("v", lambda E, ch=ch: E.tensor_scalar(
                        ot[:], ot[:], vec[:, 8 + ch:8 + ch + 1], None, ALU.add), inc="V")
                    P.op("y", lambda E, r=r, ch=ch, tt_=tt_:
                         E.dma_start(zout[r, ch * 128:(ch + 1) * 128,
                                          tt_ * 512:(tt_ + 1) * 512], ot[:]),
                         inc="ST", waits=[a4])

    for r in range(NR):
        _row(r)


def build_l1f_program():
    P = Prog()
    P.alloc_psum_pool()
    P.alloc_spacers()
    xin = P.dram("x3T", [NR, D, T], F32, kind="ExternalInput")
    xout = P.dram("x4T", [NR, D, T], F32, kind="ExternalOutput")
    zscr = P.dram("zscr", [NR, D, T], F32, kind="Internal")
    yscr = P.dram("yscr", [NR, D, T], F32, kind="Internal")
    hscr = P.dram("hscr", [4096, TT], BF16, kind="Internal")
    C = declare_consts(P, l0=False)
    W = declare_l1f_weights(P)
    with P.scope():
        emit_l1_ffn(P, W, C, xin, zscr, hscr, yscr)
    with P.scope():
        emit_post_ln(P, C, zscr, xout, "ln3")
    P.barrier()
    P.wait("y", P.tok("ST"))
    return P.build()



def build_l0f_program():
    P = Prog()
    P.alloc_psum_pool()
    P.alloc_spacers()
    xin = P.dram("x1T", [NR, D, T], F32, kind="ExternalInput")
    xout = P.dram("x2T", [NR, D, T], F32, kind="ExternalOutput")
    zscr = P.dram("zscr", [NR, D, T], F32, kind="Internal")
    hscr = P.dram("hscr", [4096, TT], BF16, kind="Internal")
    C = declare_consts(P)
    W = declare_l0f_weights(P)
    with P.scope():
        emit_l0_ffn(P, W, C, xin, zscr, hscr)
    with P.scope():
        emit_post_ln(P, C, zscr, xout, "ln1")
    P.barrier()
    P.wait("y", P.tok("ST"))
    return P.build()


def build_l0a_program(stage=99):
    P = Prog()
    P.alloc_psum_pool()
    P.alloc_spacers()
    xin = P.dram("xT", [NR, D, T], F32, kind="ExternalInput")
    xout = P.dram("x1T", [NR, D, T], F32, kind="ExternalOutput")
    zscr = P.dram("zscr", [NR, D, T], F32, kind="Internal")
    C = declare_consts(P)
    W = declare_l0a_weights(P)
    with P.scope():
        import os
        emit_l0_attn(P, W, C, xin, zscr if stage >= 10 else xout, stage=stage,
                     sub=int(os.environ.get('KSUB', '9')))
    if stage >= 10:
        with P.scope():
            emit_post_ln(P, C, zscr, xout, "ln0")
    P.barrier()
    P.wait("y", P.tok("ST"))
    return P.build()


def kernel(**inputs):
    consts = host_constants()
    x = np.asarray(inputs["L0_attn_x"], np.float32)
    w_l0a = prep_l0_attn_weights(inputs)
    w_l0f = prep_l0_ffn_weights(inputs)
    w_l1a = prep_l1_attn_weights(inputs)
    w_l1f = prep_l1_ffn_weights(inputs)

    progs = [
        (build_l0a_program(stage=99), "xT", "x1T", dict(**consts, **w_l0a)),
        (build_l0f_program(), "x1T", "x2T", dict(**consts, **w_l0f)),
        (build_l1a_program(), "x2T", "x3T", dict(**consts, **w_l1a)),
        (build_l1f_program(), "x3T", "x4T", dict(**consts, **w_l1f)),
    ]
    cur = [np.ascontiguousarray(x[2 * core:2 * core + 2].transpose(0, 2, 1))
           for core in range(8)]
    for nc, kin, kout, wmap in progs:
        in_maps = []
        for core in range(8):
            m = dict(wmap)
            m[kin] = cur[core]
            in_maps.append(m)
        res = run_bass_kernel_spmd(nc, in_maps, core_ids=list(range(8)))
        cur = [np.asarray(res.results[core][kout]) for core in range(8)]
    out = np.concatenate([c.transpose(0, 2, 1) for c in cur], axis=0)
    return np.ascontiguousarray(out)
